# version 24 (final)
# baseline (speedup 1.0000x reference)
import os
import numpy as np
import ml_dtypes
import contextlib
import concourse.bass as bass
import concourse.mybir as mybir
from concourse.bass import IndirectOffsetOnAxis
from concourse.bass_utils import run_bass_kernel_spmd

F32 = mybir.dt.float32
BF16 = mybir.dt.bfloat16
I32 = mybir.dt.int32
AF = mybir.ActivationFunctionType
ALU = mybir.AluOpType
AX = mybir.AxisListType

T = 2048
D = 1024
NH = 16
HD = 64
NP = 8
H = 2048
E = 8
C = 1024
NTT = T // 128
EPS = 1e-5
SCALE = 0.125
CAP = 30.0

PAIRS = [(0, 4), (1, 5), (2, 6), (3, 7), (8, 12), (9, 13), (10, 14), (11, 15)]
DBGS = int(os.environ.get("MK_DBGS", "0"))


def _win(qt):
    kc0 = max(0, qt - 8)
    return kc0, qt - kc0 + 1


class MK:
    def __init__(self, nc):
        self.nc = nc
        self.ctx = contextlib.ExitStack()
        n = self.ctx.enter_context

        d = nc.dram_tensor
        self.d_x = d("x", [T, D], F32, kind="ExternalInput").ap()
        self.d_wqT = d("wqT", [NP, 128, 1024], F32, kind="ExternalInput").ap()
        self.d_wkT = d("wkT", [128, 2048], F32, kind="ExternalInput").ap()
        self.d_wvT = d("wvT", [128, 2048], F32, kind="ExternalInput").ap()
        self.d_woT = d("woT", [128, 8192], F32, kind="ExternalInput").ap()
        self.d_wg = d("wg", [128, 64], F32, kind="ExternalInput").ap()
        self.d_w1s = d("w1s", [E, 16, 128, 1024], F32, kind="ExternalInput").ap()
        self.d_w3s = d("w3s", [E, 16, 128, 1024], F32, kind="ExternalInput").ap()
        self.d_w2 = d("w2", [E, 128, 16384], F32, kind="ExternalInput").ap()
        self.d_id32 = d("id32", [128, 128], F32, kind="ExternalInput").ap()
        self.d_idbf = d("idbf", [128, 128], BF16, kind="ExternalInput").ap()
        self.d_maskc = d("maskc", [128, 128], F32, kind="ExternalInput").ap()
        self.d_maska = d("maska", [128, 128], F32, kind="ExternalInput").ap()
        self.d_triu = d("triu", [128, 128], F32, kind="ExternalInput").ap()
        self.d_tokid = d("tokid", [128, 128], F32, kind="ExternalInput").ap()
        self.d_eCr = d("eCr", [1, 128], F32, kind="ExternalInput").ap()
        self.d_out = d("out", [T, D], F32, kind="ExternalOutput").ap()
        if DBGS:
            self.d_dbgi = d("dbgi", [128, 64], I32, kind="ExternalOutput").ap()
            self.d_dbgp = d("dbgp", [128, 128], I32, kind="ExternalOutput").ap()
            self.d_dbgg = d("dbgg", [128, 1024], BF16, kind="ExternalOutput").ap()
            self.d_dbgm2 = d("dbgm2", [128, 64], F32, kind="ExternalOutput").ap()
            self.d_dbgr = d("dbgr", [64, 128], F32, kind="ExternalOutput").ap()
        self.d_xnat = d("xnat", [T + 1, D], BF16).ap()
        self.dh_mtab = d("mtab", [T + 1, E], F32)
        self.d_mtab = self.dh_mtab.ap()
        self.d_idx = d("idxl", [E * C, 1], F32).ap()
        self.d_yacc = d("yacc", [T + 1, D], F32).ap()

        sb = lambda name, shape, dt: n(nc.sbuf_tensor(name, shape, dt))
        self.ACC = sb("ACCb", [128, NTT * 1024], F32)
        self.XNT = sb("XNTb", [128, 8 * 2048], BF16)
        self.AR = sb("ARb", [128, 51200], BF16)
        self.id32 = sb("id32b", [128, 128], F32)
        self.idbf = sb("idbfb", [128, 128], BF16)
        self.maskc = sb("maskcb", [128, 128], F32)
        self.maska = sb("maskab", [128, 128], F32)
        self.triu = sb("triub", [128, 128], F32)
        self.tokid = sb("tokidb", [128, 128], F32)
        self.wgt = sb("wgtb", [128, 64], F32)
        self.stats = sb("statsb", [128, 128], F32)
        self.Lsb = sb("Lsbb", [128, 128], F32)
        self.eq1 = sb("eq1b", [128, 128], F32)
        self.eq2 = sb("eq2b", [128, 128], F32)
        self.msb = sb("msbb", [128, 128], F32)
        self.l2t = sb("l2tb", [128, 8], F32)
        self.p12 = sb("p12b", [128, 32], F32)
        self.sums = sb("sumsb", [128, 4], F32)
        self.csts = sb("cstsb", [128, 2], F32)
        self.junk2 = sb("junk2b", [128, 8], F32)
        self.onesb = sb("onesbb", [128, 1], BF16)
        self.onesp = sb("onespb", [128, 64], BF16)
        self.zbuf = sb("zbufb", [128, 512], F32)
        self.onesc = sb("onescb", [128, 1], F32)
        self.onesr = sb("onesrb", [1, 128], F32)
        self.totsb = sb("totsbb", [1, 128], F32)
        self.basesb = sb("basesbb", [1, 128], F32)
        self.eCr = sb("eCrb", [1, 128], F32)
        self.idxld = sb("idxldb", [64, 128], F32)
        self.idx = sb("idxb", [128, 64], I32)
        self.ppint = sb("ppintb", [128, 128], I32)
        self.mgA = [sb(f"mgA{i}", [128, 64], F32) for i in range(2)]

        def av(off, cols, dt=BF16):
            if dt == BF16:
                return self.AR[:, off // 2: off // 2 + cols]
            assert dt == F32
            return self.AR[:, off // 2: off // 2 + 2 * cols].bitcast(F32)

        self.KT = [av(0 + i * 4096, 2048) for i in range(2)]
        self.Vt = [av(8192 + tt * 528 if tt < 15 else 96768, 260)
                    for tt in range(NTT)]
        self.OT = [av(16384 + m * 4096, 2048) for m in range(NP)]
        self.xn_scr = [av(16384 + i * 4096, 1024, F32) for i in range(2)]
        self.xnT32 = [av(24576 + i * 512, 128, F32) for i in range(2)]
        self.junk = av(32768, 1024, F32)
        self.woT = [av(49152 + ic * 2048, 1024) for ic in range(8)]
        self.wkT = av(65536, 2048)
        self.wvT = av(69632, 2048)
        self.wqT = [av(73728 + i * 2048, 1024) for i in range(2)]
        self.QT = [av(77824 + i * 4096, 2048) for i in range(2)]
        self.PTA = [[av(86016 + (h * 4 + r) * 1024, 512) for r in range(4)]
                    for h in range(2)]
        rb = av(94720, 1024)
        self.rA = rb[64:65, 0:512]
        self.rB = rb[64:65, 512:1024]
        self.tmpB = [rb[0:64, 0:512], rb[0:64, 512:1024],
                     av(97792, 512)[0:64, :], av(101376, 512)[0:64, :]]
        rs = av(98816, 1024)
        self.RsbA = rs[0:64, 0:512]
        self.RsbB = rs[0:64, 512:1024]
        self.l2s = av(94208, 128, F32)

        self.HT = [av(0 + jt * 2048, 1024) for jt in range(16)]
        self.W2 = [av(32768 + hc * 2048, 1024) for hc in range(16)]
        self.W1S = [av(65536 + i * 2048, 1024) for i in range(2)]
        self.W3S = [av(69632 + i * 2048, 1024) for i in range(2)]
        self.SIL = [av(73728 + i * 2048, 512, F32) for i in range(2)]
        self.YSC = [av(77824 + i * 4096, 1024, F32) for i in range(2)]
        self.XG = [av(86016 + k * 2048, 1024) for k in range(8)]
        self.XGT = [self.XNT[:, 0:8192], self.XNT[:, 8192:16384]]

        self.PS = n(nc.psum_tensor("PSall", [128, 4096], F32))
        self.ps = lambda c0, cols: self.PS[:, c0:c0 + cols]
        self.psb = lambda c0, cols: self.PS.ap().bitcast(BF16)[:, 2 * c0: 2 * c0 + cols]

        sems = ("sX sC sCm sWg sNM sSq sVar sRstd sXn sTr1 sTrEvV sTrEvS sGmm sGe "
                "sSelfV sSig sM sKTt sKTe sVt sVe sQTt sQTe sS sTanh sMask sExp sOTd "
                "sX2 sSm sRe sRcp2 sRb sNorm sPVk sWoT sWoV sH1 sH3 sSil sHTm sB "
                "sSelfS sTouch sOut "
                "sSel sTot sTotE sBase sPP sPPe sMdone sMt sZs sXd sScat sIdxL "
                "sIdxT sIdxE sGa sXgT sXgE sXgEv sYsc sYs sYal sFin sXnD sYsD").split()
        for s in sems:
            setattr(self, s, n(nc.semaphore(s)))
        self.selfv = 0

        self.loads = []
        self.lidx = {}
        def L(key):
            self.lidx[key] = len(self.loads)
            self.loads.append(key)
        L(("wkT",)); L(("wvT",)); L(("woT",))
        L(("wqT", 0)); L(("wqT", 1))
        L(("fills",))
        for tt in range(NTT):
            L(("xnd", tt))
        L(("xnz",))
        for m in range(2, NP):
            L(("wqT", m))
        for tt in range(NTT):
            for e in range(E):
                L(("scat", tt, e))
        L(("idxrb",))
        L(("mtd",))
        for jt in range(2):
            L(("w1s", 0, jt)); L(("w3s", 0, jt))
        for k in range(8):
            L(("xg", 0, k)); L(("mg", 0, k))
        for jt in range(2, 8):
            L(("w1s", 0, jt)); L(("w3s", 0, jt))
        for k in range(8):
            L(("xg", 1, k)); L(("mg", 1, k))
        for jt in range(8, 16):
            L(("w1s", 0, jt)); L(("w3s", 0, jt))
        L(("w2", 0))
        for e in range(1, E):
            for jt in range(2):
                L(("w1s", e, jt)); L(("w3s", e, jt))
            for k in range(8):
                L(("ysc", e - 1, k))
            if e + 1 < E:
                for k in range(8):
                    L(("xg", e + 1, k)); L(("mg", e + 1, k))
            for jt in range(2, 16):
                L(("w1s", e, jt)); L(("w3s", e, jt))
            L(("w2", e))
        for k in range(8):
            L(("ysc", E - 1, k))
        for tt in range(NTT):
            L(("yrb", tt))

        self.wgidx = {}
        wi = 0
        for key in self.loads:
            if key[0] in ("wkT", "wvT", "woT", "wqT", "w1s", "w3s", "w2"):
                self.wgidx[key] = wi
                wi += 1
        self.gaidx = {}
        gi = 0
        for key in self.loads:
            if key[0] in ("xg", "mg"):
                self.gaidx[key] = gi
                gi += 1

        self.AI = []
        self.AK = []
        self.MCinst = {}
        self.glastK = {}
        slot_last = {}
        mcount = 0
        for m in range(NP):
            for c in range(4):
                kcl = list(range(max(0, 4 * c - 8), 4 * c + 4))
                nsets = len(kcl) // 2
                sets = [(kcl[2 * t], kcl[2 * t + 1]) for t in range(nsets)]
                tf = next(t for t, p in enumerate(sets) if p[0] == 4 * c)
                order = [sets[tf]] + [p for t, p in enumerate(sets) if t != tf]
                mcl = []
                for h in (0, 1):
                    for t, pair in enumerate(order):
                        i = len(self.AI)
                        kops = []
                        for j, kc in enumerate(pair):
                            qlo = max(4 * c, kc)
                            qhi = min(4 * c + 3, kc + 8)
                            if 4 * c <= kc <= 4 * c + 3:
                                mask, maskq = 'd', kc - 4 * c
                            elif kc <= 4 * c - 5:
                                mask, maskq = 'a', kc + 8 - 4 * c
                            else:
                                mask, maskq = None, None
                            if mask:
                                mcount += 1
                            k = len(self.AK)
                            self.AK.append(dict(
                                k=k, i=i, j=j, m=m, c=c, h=h, kc=kc, qlo=qlo, qhi=qhi,
                                mask=mask, maskq=maskq, mcount=mcount,
                                slotprev=slot_last.get((h, kc % 4)),
                                gfirst=(t == 0 and j == 0),
                                glast=(t == nsets - 1 and j == 1),
                                pv_first=(h == 0 and t == 0 and j == 0),
                                pv_last=(h == 1 and t == nsets - 1 and j == 1)))
                            slot_last[(h, kc % 4)] = k
                            if t == nsets - 1 and j == 1:
                                self.glastK[(m, c, h)] = k
                            kops.append(k)
                        self.AI.append(dict(m=m, c=c, h=h, kcs=pair, i=i, kops=kops))
                        mcl.append(i)
                self.MCinst[(m, c)] = mcl
        self.pair_last = {}
        self.pair_lastP0 = {}
        for inst in self.AI:
            self.pair_last[inst['m']] = inst['i']
            if inst['i'] % 3 == 0:
                self.pair_lastP0[inst['m']] = inst['i']

    def wld(self, eng, key):
        eng.wait_ge(self.sWg, 16 * (self.wgidx[key] + 1))

    @staticmethod
    def u3(m, qt, h):
        return (m * 16 + qt) * 2 + h

    @staticmethod
    def a2(e, jt):
        return e * 16 + jt

    @staticmethod
    def b2(e, k):
        return e * 8 + k

    def emit_sync(self, sy):
        for tt in range(NTT):
            sy.dma_start(self.ACC[:, tt * 1024:(tt + 1) * 1024],
                         self.d_x[tt * 128:(tt + 1) * 128, :]).then_inc(self.sX, 16)
        sy.dma_start(self.id32[:, :], self.d_id32).then_inc(self.sC, 16)
        sy.dma_start(self.idbf[:, :], self.d_idbf).then_inc(self.sC, 16)
        sy.dma_start(self.maskc[:, :], self.d_maskc).then_inc(self.sC, 16)
        sy.dma_start(self.maska[:, :], self.d_maska).then_inc(self.sC, 16)
        sy.dma_start(self.wgt[:, :], self.d_wg).then_inc(self.sC, 16)
        sy.dma_start(self.triu[:, :], self.d_triu).then_inc(self.sC, 16)
        sy.dma_start(self.tokid[:, :], self.d_tokid).then_inc(self.sC, 16)
        sy.dma_start(self.eCr[:, :], self.d_eCr).then_inc(self.sC, 16)
        for mc in range(4 * NP):
            m, c = divmod(mc, 4)
            sy.wait_ge(self.sNorm, mc + 1)
            sy.dma_start(self.OT[m][64:128, c * 512:(c + 1) * 512],
                         self.tmpB[mc % 4]).then_inc(self.sOTd, 16)

        if DBGS:
            sy.wait_ge(self.sIdxE, 1)
            sy.dma_start(self.d_dbgi, self.idx[:, :]).then_inc(self.sOut, 16)
            sy.dma_start(self.d_dbgr, self.idxld[:, :]).then_inc(self.sOut, 16)
            sy.wait_ge(self.sPPe, 1)
            sy.dma_start(self.d_dbgp, self.ppint[:, :]).then_inc(self.sOut, 16)
            sy.wait_ge(self.sGa, 16 * 2)
            sy.dma_start(self.d_dbgg, self.XG[0]).then_inc(self.sOut, 16)
            sy.wait_ge(self.sGa, 16 * 16)
            sy.dma_start(self.d_dbgm2, self.mgA[0][:, :]).then_inc(self.sOut, 16)
        sy.wait_ge(self.sZs, 16 * 36)
        for tt in range(NTT):
            sy.wait_ge(self.sFin, tt + 1)
            sy.wait_ge(self.sTouch, tt + 1)
            sy.dma_start(self.d_out[tt * 128:(tt + 1) * 128, :],
                         self.ACC[:, tt * 1024:(tt + 1) * 1024]).then_inc(self.sOut, 16)
        sy.wait_ge(self.sOut, 16 * (NTT + (5 if DBGS else 0)))

    def emit_gpsimd(self, g):
        bnd = []
        for e in range(E):
            r = g.alloc_register(f"bnd{e}")
            g.reg_mov(r, (e + 1) * C - 1)
            bnd.append(r)
        for key in self.loads:
            kind = key[0]
            if kind == "wkT":
                g.dma_start(self.wkT, self.d_wkT).then_inc(self.sWg, 16)
            elif kind == "wvT":
                g.dma_start(self.wvT, self.d_wvT).then_inc(self.sWg, 16)
            elif kind == "woT":
                g.dma_start(self.AR[:, 49152 // 2: 49152 // 2 + 8192],
                            self.d_woT).then_inc(self.sWg, 16)
            elif kind == "wqT":
                m = key[1]
                if m >= 2:
                    g.wait_ge(self.sQTt, 4 * (m - 1))
                g.dma_start(self.wqT[m % 2], self.d_wqT[m]).then_inc(self.sWg, 16)
            elif kind == "xnd":
                tt = key[1]
                g.wait_ge(self.sXnD, tt + 1)
                g.dma_start(self.d_xnat[tt * 128:(tt + 1) * 128, :],
                            self.xn_scr[tt % 2]).then_inc(self.sXd, 16)
            elif kind == "xnz":
                g.wait_ge(self.sCm, 8)
                g.dma_start(self.d_xnat[T:T + 1, 0:512],
                            self.zbuf[0:1, 0:512]).then_inc(self.sXd, 16)
                g.dma_start(self.d_xnat[T:T + 1, 512:1024],
                            self.zbuf[0:1, 0:512]).then_inc(self.sXd, 16)
            elif kind == "scat":
                tt, e = key[1], key[2]
                if (tt, e) == (0, 0):
                    g.wait_ge(self.sPPe, 1)
                    g.wait_ge(self.sZs, 16)
                col = tt * 8 + e
                g.indirect_dma_start(
                    self.d_idx,
                    IndirectOffsetOnAxis(self.ppint[:, col:col + 1], 0),
                    self.tokid[:, col:col + 1], None,
                    bounds_check=bnd[e],
                    oob_is_err=False).then_inc(self.sScat, 16)
            elif kind == "fills":
                g.wait_ge(self.sCm, 7)
                g.dma_start(self.d_idx.rearrange("(p c) one -> p (c one)", p=128),
                            self.zbuf[:, 0:64]).then_inc(self.sZs, 16)
                g.wait_ge(self.sCm, 8)
                for r in range(NTT):
                    for hh in range(2):
                        g.dma_start(self.d_yacc[r * 128:(r + 1) * 128,
                                                hh * 512:(hh + 1) * 512],
                                    self.zbuf[:, :]).then_inc(self.sZs, 16)
                g.dma_start(self.d_yacc[T:T + 1, 0:512],
                            self.zbuf[0:1, :]).then_inc(self.sZs, 16)
                g.dma_start(self.d_yacc[T:T + 1, 512:1024],
                            self.zbuf[0:1, :]).then_inc(self.sZs, 16)
                g.dma_start(self.d_mtab[T:T + 1, :],
                            self.zbuf[0:1, 0:E]).then_inc(self.sZs, 16)
            elif kind == "mtd":
                g.wait_ge(self.sMdone, 1)
                g.dma_start(bass.AP(self.dh_mtab, 0, [[E, 128], [128 * E, NTT], [1, E]]),
                            self.msb[:, :].rearrange("p (tt e) -> p tt e",
                                                     e=E)).then_inc(self.sMt, 16)
            elif kind == "idxrb":
                g.wait_ge(self.sScat, 16 * NTT * E)
                g.dma_start(self.idxld[:, :],
                            self.d_idx.rearrange("(pp c) one -> pp (c one)",
                                                 pp=64)).then_inc(self.sIdxL, 16)
            elif kind == "yrb":
                tt = key[1]
                if tt == 0:
                    g.wait_ge(self.sYs, 16 * 8 * E)
                if tt >= 2:
                    g.wait_ge(self.sFin, tt - 1)
                g.dma_start(self.YSC[tt % 2],
                            self.d_yacc[tt * 128:(tt + 1) * 128, :]).then_inc(self.sYal, 16)
            elif kind == "xg":
                e, k = key[1], key[2]
                if (e, k) == (0, 0):
                    g.wait_ge(self.sIdxE, 1)
                    g.wait_ge(self.sXd, 16 * 18)
                    g.wait_ge(self.sOTd, 16 * 4 * NP)
                if e >= 1:
                    g.wait_ge(self.sXgT, 8 * (e - 1) + k + 1)
                g.indirect_dma_start(
                    self.XG[k], None, self.d_xnat,
                    IndirectOffsetOnAxis(self.idx[:, e * 8 + k: e * 8 + k + 1], 0)
                ).then_inc(self.sGa, 16)
            elif kind == "mg":
                e, k = key[1], key[2]
                if (e, k) == (0, 0):
                    g.wait_ge(self.sMt, 16)
                if e >= 2:
                    g.wait_ge(self.sYsc, 8 * (e - 1))
                g.indirect_dma_start(
                    self.mgA[e % 2][:, k * 8:(k + 1) * 8], None, self.d_mtab,
                    IndirectOffsetOnAxis(self.idx[:, e * 8 + k: e * 8 + k + 1], 0)
                ).then_inc(self.sGa, 16)
            elif kind == "ysc":
                e, k = key[1], key[2]
                b = self.b2(e, k)
                g.wait_ge(self.sYsD, b + 1)
                if k == 0 and e == 0:
                    g.wait_ge(self.sZs, 16 * 36)
                if k == 0 and e >= 1:
                    g.wait_ge(self.sYs, 16 * 8 * e)
                g.indirect_dma_start(
                    self.d_yacc,
                    IndirectOffsetOnAxis(self.idx[:, e * 8 + k: e * 8 + k + 1], 0),
                    self.YSC[b % 2], None,
                    compute_op=ALU.add).then_inc(self.sYs, 16)
            elif kind == "w1s" or kind == "w3s":
                e, jt = key[1], key[2]
                a = self.a2(e, jt)
                if a < 2:
                    g.wait_ge(self.sKTt, 8)
                    g.wait_ge(self.sVt, NTT)
                else:
                    g.wait_ge(self.sH3, 2 * (a - 2) + 2)
                src = self.d_w1s if kind == "w1s" else self.d_w3s
                dst = (self.W1S if kind == "w1s" else self.W3S)[jt % 2]
                g.dma_start(dst, src[e, jt]).then_inc(self.sWg, 16)
            elif kind == "w2":
                e = key[1]
                if e == 0:
                    g.wait_ge(self.sWoT, NTT)
                else:
                    g.wait_ge(self.sB, 8 * e)
                g.dma_start(self.AR[:, 32768 // 2: 32768 // 2 + 16384],
                            self.d_w2[e]).then_inc(self.sWg, 16)

    def emit_scalar(self, s):
        for tt in range(NTT):
            s.wait_ge(self.sNM, tt + 1)
            s.activation(self.junk, self.ACC[:, tt * 1024:(tt + 1) * 1024],
                         AF.Square, bias=self.stats[:, 16 + tt:17 + tt],
                         accum_out=self.stats[:, 32 + tt:33 + tt]).then_inc(self.sSq, 1)
        s.wait_ge(self.sVar, 1)
        s.wait_ge(self.sCm, 2)
        s.activation(self.stats[:, 48:64], self.stats[:, 48:64], AF.Ln,
                     bias=self.csts[:, 0:1]).then_inc(self.sSelfS, 1)
        s.wait_ge(self.sSelfS, 1)
        s.activation(self.stats[:, 48:64], self.stats[:, 48:64], AF.Exp,
                     scale=-0.5).then_inc(self.sRstd, 1)
        s.wait_ge(self.sM, 1)
        s.activation(self.p12[:, 0:16], self.stats[:, 80:96], AF.Sigmoid, scale=-1.0)
        s.activation(self.p12[:, 16:32], self.stats[:, 80:96], AF.Sigmoid).then_inc(self.sSig, 1)
        for p0 in range(0, len(self.AI), 2):
            for i in (p0, p0 + 1):
                inst = self.AI[i]
                s.wait_ge(self.sS, i + 1)
                bb = 1024 * (i % 3)
                for j, k in enumerate(inst['kops']):
                    ak = self.AK[k]
                    c = ak['c']
                    off = bb + j * 512 + (ak['qlo'] - 4 * c) * 128
                    w = (ak['qhi'] - ak['qlo'] + 1) * 128
                    s.activation(self.ps(off, w), self.ps(off, w), AF.Tanh,
                                 scale=SCALE / CAP).then_inc(self.sTanh, 1)
            for i in (p0, p0 + 1):
                inst = self.AI[i]
                bb = 1024 * (i % 3)
                for j, k in enumerate(inst['kops']):
                    ak = self.AK[k]
                    c, h, kc = ak['c'], ak['h'], ak['kc']
                    s.wait_ge(self.sTanh, k + 1)
                    if ak['mask']:
                        s.wait_ge(self.sMask, ak['mcount'])
                    if ak['slotprev'] is not None:
                        s.wait_ge(self.sPVk, ak['slotprev'] + 1)
                    off = bb + j * 512 + (ak['qlo'] - 4 * c) * 128
                    qrel = (ak['qlo'] - 4 * c) * 128
                    w = (ak['qhi'] - ak['qlo'] + 1) * 128
                    s.activation(self.PTA[h][kc % 4][:, qrel:qrel + w],
                                 self.ps(off, w), AF.Exp,
                                 scale=CAP, bias=self.csts[:, 1:2]).then_inc(self.sExp, 1)
        for e in range(E):
            for k in range(0, 8, 2):
                s.wait_ge(self.sXgT, 8 * e + k + 1)
                for dc in range(8):
                    op = s.copy(self.XGT[e % 2][:, dc * 1024 + k * 128: dc * 1024 + (k + 1) * 128],
                                self.psb(512 * (k % 2), 1024)[:, dc * 128:(dc + 1) * 128])
                op.then_inc(self.sXgE, 1)
            for jt in range(16):
                a = self.a2(e, jt)
                for tcl in range(2):
                    s.wait_ge(self.sH1, 2 * a + tcl + 1)
                    if a >= 1:
                        s.wait_ge(self.sHTm, 2 * (a - 1) + tcl + 1)
                    s.activation(self.SIL[tcl], self.ps(2048 * (jt % 2) + 512 * tcl, 512),
                                 AF.Silu).then_inc(self.sSil, 1)

    def emit_vector(self, v):
        def selfwait(op):
            op.then_inc(self.sSelfV, 1)
            self.selfv += 1
            v.wait_ge(self.sSelfV, self.selfv)
        v.memset(self.csts[:, 0:1], EPS)
        v.memset(self.csts[:, 1:2], -CAP)
        v.memset(self.onesc[:, :], 1.0)
        v.memset(self.onesb[:, :], 1.0)
        v.memset(self.onesr[:, :], 1.0)
        v.memset(self.onesp[:, :], 1.0)
        v.memset(self.zbuf[:, :], 2048.0)
        op = v.tensor_copy(self.junk2[:, 0:2], self.zbuf[:, 510:512])
        op.then_inc(self.sCm, 7)
        v.wait_ge(self.sZs, 16)
        v.memset(self.zbuf[:, :], 0.0)
        v.tensor_copy(self.junk2[:, 0:2], self.zbuf[:, 510:512]).then_inc(self.sCm, 1)
        for tt in range(NTT):
            for kv in range(4):
                v.memset(self.Vt[tt][:, 65 * kv + 64:65 * kv + 65], 1.0)
        for tt in range(NTT):
            v.wait_ge(self.sX, 16 * (tt + 1))
            acct = self.ACC[:, tt * 1024:(tt + 1) * 1024]
            op = v.reduce_sum(self.stats[:, tt:tt + 1], acct, AX.X)
            selfwait(op)
            v.tensor_scalar(self.stats[:, 16 + tt:17 + tt], self.stats[:, tt:tt + 1],
                            -1.0 / D, None, ALU.mult).then_inc(self.sNM, 1)
        v.wait_ge(self.sSq, NTT)
        v.tensor_scalar(self.stats[:, 48:64], self.stats[:, 32:48],
                        1.0 / D, None, ALU.mult).then_inc(self.sVar, 1)
        def emit_xn(tt):
            v.wait_ge(self.sRstd, 1)
            if tt >= 2:
                v.wait_ge(self.sTr1, (tt - 2) * 8 + 8)
                v.wait_ge(self.sXd, 16 * (tt - 1))
            acct = self.ACC[:, tt * 1024:(tt + 1) * 1024]
            v.tensor_scalar(self.xn_scr[tt % 2], acct,
                            self.stats[:, 16 + tt:17 + tt], self.stats[:, 48 + tt:49 + tt],
                            ALU.add, ALU.mult).then_inc(self.sXn, 1)
            v.tensor_copy(self.junk2[:, 4:6], self.xn_scr[tt % 2][:, 1022:1024]).then_inc(self.sXnD, 1)
        emit_xn(0)
        emit_xn(1)
        for tt in range(NTT):
            for dc in range(8):
                i = tt * 8 + dc
                v.wait_ge(self.sTr1, i + 1)
                if i >= 2:
                    v.wait_ge(self.sGmm, i - 1)
                v.tensor_copy(self.xnT32[dc % 2], self.ps(512 * (dc % 2), 128)).then_inc(self.sTrEvV, 1)
                v.tensor_copy(self.XNT[:, dc * 2048 + tt * 128: dc * 2048 + (tt + 1) * 128],
                              self.ps(512 * (dc % 2), 128)).then_inc(self.sTrEvS, 1)
            v.wait_ge(self.sGmm, tt * 8 + 8)
            v.tensor_copy(self.Lsb[:, tt * 8:(tt + 1) * 8],
                          self.ps(1024 + 512 * (tt % 2), 8)).then_inc(self.sGe, 1)
            if tt + 2 < NTT:
                emit_xn(tt + 2)
        for tt in range(NTT):
            op = v.reduce_max(self.stats[:, 64 + tt:65 + tt],
                              self.Lsb[:, tt * 8:(tt + 1) * 8], AX.X)
        selfwait(op)
        for tt in range(NTT):
            op = v.tensor_scalar(self.eq1[:, tt * 8:(tt + 1) * 8],
                                 self.Lsb[:, tt * 8:(tt + 1) * 8],
                                 self.stats[:, 64 + tt:65 + tt], None, ALU.is_equal)
        selfwait(op)
        for tt in range(NTT):
            op = v.scalar_tensor_tensor(self.l2s[:, tt * 8:(tt + 1) * 8],
                                        self.eq1[:, tt * 8:(tt + 1) * 8],
                                        -1e30, self.Lsb[:, tt * 8:(tt + 1) * 8],
                                        ALU.mult, ALU.add)
        selfwait(op)
        for tt in range(NTT):
            op = v.reduce_max(self.stats[:, 96 + tt:97 + tt],
                              self.l2s[:, tt * 8:(tt + 1) * 8], AX.X)
        selfwait(op)
        for tt in range(NTT):
            v.tensor_scalar(self.eq2[:, tt * 8:(tt + 1) * 8],
                            self.l2s[:, tt * 8:(tt + 1) * 8],
                            self.stats[:, 96 + tt:97 + tt], None, ALU.is_equal)
        for tt in range(NTT):
            op = v.tensor_sub(self.stats[:, 80 + tt:81 + tt],
                              self.stats[:, 96 + tt:97 + tt], self.stats[:, 64 + tt:65 + tt])
        selfwait(op)
        v.engine_nop().then_inc(self.sM, 1)
        v.wait_ge(self.sSig, 1)
        for tt in range(NTT):
            op = v.tensor_scalar(self.l2t[:, :], self.eq2[:, tt * 8:(tt + 1) * 8],
                                 self.p12[:, 16 + tt:17 + tt], None, ALU.mult)
            selfwait(op)
            op = v.scalar_tensor_tensor(self.msb[:, tt * 8:(tt + 1) * 8],
                                        self.eq1[:, tt * 8:(tt + 1) * 8],
                                        self.p12[:, tt:tt + 1], self.l2t[:, :],
                                        ALU.mult, ALU.add)
        selfwait(v.engine_nop())
        v.tensor_copy(self.junk2[:, 2:4], self.msb[:, 126:128]).then_inc(self.sMdone, 1)
        v.tensor_add(self.eq1[:, :], self.eq1[:, :], self.eq2[:, :]).then_inc(self.sSel, 1)
        for it in range(2):
            for tc in range(4):
                i = it * 4 + tc
                v.wait_ge(self.sKTt, i + 1)
                v.tensor_copy(self.KT[it][:, tc * 512:(tc + 1) * 512],
                              self.ps(512 * (i % 2), 512)).then_inc(self.sKTe, 1)
        for tt in range(NTT):
            v.wait_ge(self.sVt, tt + 1)
            for kv in range(4):
                op = v.tensor_copy(self.Vt[tt][:, 65 * kv:65 * kv + 64],
                                   self.ps(1024 + 512 * (tt % 2), 256)[:, 64 * kv:64 * kv + 64])
            op.then_inc(self.sVe, 1)
        for pp in range(2):
            for tc in range(4):
                v.wait_ge(self.sQTt, 4 * pp + tc + 1)
                v.tensor_copy(self.QT[pp % 2][:, tc * 512:(tc + 1) * 512],
                              self.ps(0, 512)).then_inc(self.sQTe, 1)
        v.wait_ge(self.sTot, 1)
        op = v.tensor_copy(self.totsb[:, :], self.ps(512, 128)[0:1, :])
        selfwait(op)
        op = v.tensor_copy(self.basesb[:, :], self.eCr[:, :])
        selfwait(op)
        for tt in range(1, NTT):
            op = v.tensor_add(self.basesb[0:1, tt * 8:(tt + 1) * 8],
                              self.basesb[0:1, (tt - 1) * 8:tt * 8],
                              self.totsb[0:1, (tt - 1) * 8:tt * 8])
            selfwait(op)
        v.engine_nop().then_inc(self.sBase, 1)
        v.wait_ge(self.sPP, 1)
        op = v.tensor_scalar(self.l2s[:, :], self.ps(0, 128), 1.0e6, None, ALU.add)
        selfwait(op)
        op = v.scalar_tensor_tensor(self.l2s[:, :], self.eq1[:, :], -1.0e6,
                                    self.l2s[:, :], ALU.mult, ALU.add)
        selfwait(op)
        op = v.tensor_copy(self.ppint[:, :], self.l2s[:, :])
        selfwait(op)
        v.reduce_max(self.junk2[:, 7:8], self.ppint[:, :], AX.X).then_inc(self.sPPe, 1)
        for m in range(NP):
            for c in range(4):
                mc = m * 4 + c
                insts = self.MCinst[(m, c)]
                for i in insts:
                    bb = 1024 * (i % 3)
                    for j, k in enumerate(self.AI[i]['kops']):
                        ak = self.AK[k]
                        if not ak['mask']:
                            continue
                        v.wait_ge(self.sTanh, k + 1)
                        msk = self.maskc if ak['mask'] == 'd' else self.maska
                        off = bb + j * 512 + ak['maskq'] * 128
                        v.tensor_add(self.ps(off, 128), self.ps(off, 128),
                                     msk[:, :]).then_inc(self.sMask, 1)
                if c == 3 and m + 2 < NP:
                    pp = m + 2
                    for tc in range(4):
                        v.wait_ge(self.sQTt, 4 * pp + tc + 1)
                        v.wait_ge(self.sS, self.pair_last[pp - 2] + 1)
                        v.tensor_copy(self.QT[pp % 2][:, tc * 512:(tc + 1) * 512],
                                      self.ps(0, 512)).then_inc(self.sQTe, 1)
                obA = 3072
                obB = 3584
                p = insts[-1] % 3
                with self.nc.allow_low_precision(reason="bf16 softmax recip (bar 2.57e-3)"):
                    v.wait_ge(self.sPVk, self.glastK[(m, c, 0)] + 1)
                    v.reciprocal(self.rA, self.PS[64:65, obA:obA + 512])
                    v.wait_ge(self.sPVk, self.glastK[(m, c, 1)] + 1)
                    v.reciprocal(self.rB, self.PS[64:65, obB:obB + 512]).then_inc(self.sRcp2, 1)
                v.wait_ge(self.sRb, mc + 1)
                v.tensor_copy(self.RsbB, self.PS[0:64, 1024 * p + 512:1024 * p + 1024])
                op = v.tensor_copy(self.RsbA, self.PS[0:64, 1024 * p:1024 * p + 512])
                op.then_inc(self.sRe, 1)
                selfwait(v.engine_nop())
                if mc >= 4:
                    v.wait_ge(self.sOTd, 16 * (mc - 3))
                v.tensor_tensor(self.OT[m][0:64, c * 512:(c + 1) * 512],
                                self.PS[0:64, obA:obA + 512], self.RsbA, ALU.mult)
                v.tensor_tensor(self.tmpB[mc % 4], self.PS[0:64, obB:obB + 512],
                                self.RsbB, ALU.mult)
                op = v.tensor_copy(self.junk2[0:64, 0:2], self.tmpB[mc % 4][:, 510:512])
                op.then_inc(self.sNorm, 1)
        v.wait_ge(self.sIdxT, 1)
        op = v.tensor_copy(self.idx[:, :], self.ps(0, 64))
        selfwait(op)
        v.reduce_max(self.junk2[:, 6:7], self.idx[:, :], AX.X).then_inc(self.sIdxE, 1)
        for tt in range(NTT):
            v.wait_ge(self.sWoT, tt + 1)
            acct = self.ACC[:, tt * 1024:(tt + 1) * 1024]
            v.tensor_add(acct, acct, self.ps(1024 * (tt % 2), 1024)).then_inc(self.sWoV, 1)
        for e in range(E):
            for k in range(1, 8, 2):
                v.wait_ge(self.sXgT, 8 * e + k + 1)
                for dc in range(8):
                    op = v.tensor_copy(
                        self.XGT[e % 2][:, dc * 1024 + k * 128: dc * 1024 + (k + 1) * 128],
                        self.psb(512 * (k % 2), 1024)[:, dc * 128:(dc + 1) * 128])
                op.then_inc(self.sXgEv, 1)
            for jt in range(16):
                a = self.a2(e, jt)
                for tcl in range(2):
                    v.wait_ge(self.sSil, 2 * a + tcl + 1)
                    v.wait_ge(self.sH3, 2 * a + tcl + 1)
                    if jt == 0 and tcl == 0 and e > 0:
                        v.wait_ge(self.sB, 8 * e)
                    v.tensor_tensor(self.HT[jt][:, tcl * 512:(tcl + 1) * 512],
                                    self.SIL[tcl],
                                    self.ps(2048 * (jt % 2) + 1024 + 512 * tcl, 512),
                                    ALU.mult).then_inc(self.sHTm, 1)
            for k in range(8):
                b = self.b2(e, k)
                v.wait_ge(self.sB, b + 1)
                if b >= 2:
                    v.wait_ge(self.sYs, 16 * (b - 1))
                v.wait_ge(self.sGa, 16 * (self.gaidx[("mg", e, k)] + 1))
                v.tensor_scalar(self.YSC[b % 2], self.ps(2048 + 1024 * (b % 2), 1024),
                                self.mgA[e % 2][:, k * 8 + e: k * 8 + e + 1], None,
                                ALU.mult).then_inc(self.sYsc, 1)
                v.tensor_copy(self.junk2[:, 4:6], self.YSC[b % 2][:, 1022:1024]).then_inc(self.sYsD, 1)
        for tt in range(NTT):
            v.wait_ge(self.sYal, 16 * (tt + 1))
            acct = self.ACC[:, tt * 1024:(tt + 1) * 1024]
            v.tensor_add(acct, acct, self.YSC[tt % 2]).then_inc(self.sFin, 1)
            v.tensor_copy(self.junk2[:, 2:4], acct[:, 1022:1024]).then_inc(self.sTouch, 1)

    def emit_tensor(self, t):
        t.wait_ge(self.sC, 16 * 8)
        for tt in range(NTT):
            for dc in range(8):
                i = tt * 8 + dc
                t.wait_ge(self.sXn, tt + 1)
                if i >= 2:
                    t.wait_ge(self.sTrEvV, i - 1)
                    t.wait_ge(self.sTrEvS, i - 1)
                t.transpose(self.ps(512 * (dc % 2), 128),
                            self.xn_scr[tt % 2][:, dc * 128:(dc + 1) * 128],
                            self.id32[:, :]).then_inc(self.sTr1, 1)
                if i >= 1:
                    self._gate_mm(t, i - 1)
            if tt == NTT - 1:
                self._gate_mm(t, NTT * 8 - 1)
        t.wait_ge(self.sTrEvV, 128)
        t.wait_ge(self.sTrEvS, 128)
        t.wait_ge(self.sGe, NTT)
        self.wld(t, ("wkT",))
        for it in range(2):
            for tc in range(4):
                i = it * 4 + tc
                if i >= 2:
                    t.wait_ge(self.sKTe, i - 1)
                for dc in range(8):
                    mm = t.matmul(self.ps(512 * (i % 2), 512),
                                  self.wkT[:, dc * 256 + it * 128: dc * 256 + (it + 1) * 128],
                                  self.XNT[:, dc * 2048 + tc * 512: dc * 2048 + (tc + 1) * 512],
                                  start=(dc == 0), stop=(dc == 7))
                mm.then_inc(self.sKTt, 1)
        self.wld(t, ("wvT",))
        for tt in range(NTT):
            if tt >= 2:
                t.wait_ge(self.sVe, tt - 1)
            for dc in range(8):
                mm = t.matmul(self.ps(1024 + 512 * (tt % 2), 256),
                              self.XNT[:, dc * 2048 + tt * 128: dc * 2048 + (tt + 1) * 128],
                              self.wvT[:, dc * 256:(dc + 1) * 256],
                              start=(dc == 0), stop=(dc == 7))
            mm.then_inc(self.sVt, 1)
        for pp in range(2):
            self._qt_proj(t, pp)
        t.wait_ge(self.sSel, 1)
        t.wait_ge(self.sQTe, 8)
        t.matmul(self.PS[0:1, 512:640], self.onesc[:, :], self.eq1[:, :],
                 start=True, stop=True).then_inc(self.sTot, 1)
        t.wait_ge(self.sBase, 1)
        t.matmul(self.ps(0, 128), self.triu[:, :], self.eq1[:, :],
                 start=True, stop=False)
        t.matmul(self.ps(0, 128), self.onesr[:, :], self.basesb[:, :],
                 start=False, stop=True).then_inc(self.sPP, 1)
        t.wait_ge(self.sKTe, 8)
        t.wait_ge(self.sVe, NTT)
        t.wait_ge(self.sCm, 7)
        t.wait_ge(self.sPPe, 1)
        def emit_R(pr):
            mcp, p = pr
            t.wait_ge(self.sRcp2, mcp + 1)
            t.matmul(self.PS[0:64, 1024 * p:1024 * p + 512],
                     self.onesp[64:65, 0:64], self.rA, start=True, stop=True)
            t.matmul(self.PS[0:64, 1024 * p + 512:1024 * p + 1024],
                     self.onesp[64:65, 0:64], self.rB,
                     start=True, stop=True).then_inc(self.sRb, 1)
        pendR = None
        for m in range(NP):
            t.wait_ge(self.sQTe, 4 * min(m + 2, NP))
            ktile = self.KT[0] if m < 4 else self.KT[1]
            for c in range(4):
                mc = m * 4 + c
                insts = self.MCinst[(m, c)]
                for idx, i in enumerate(insts):
                    inst = self.AI[i]
                    if i >= 3:
                        t.wait_ge(self.sExp, 2 * (i - 3) + 2)
                    if idx == 2 and mc >= 1:
                        t.wait_ge(self.sRe, mc)
                    bb = 1024 * (i % 3)
                    base = 64 * inst['h']
                    for j, k in enumerate(inst['kops']):
                        ak = self.AK[k]
                        qlo, qhi, kc = ak['qlo'], ak['qhi'], ak['kc']
                        w = (qhi - qlo + 1) * 128
                        mm = t.matmul(self.ps(bb + j * 512 + (qlo - 4 * c) * 128, w),
                                      ktile[base:base + 64, kc * 128:(kc + 1) * 128],
                                      self.QT[m % 2][base:base + 64, qlo * 128:(qhi + 1) * 128],
                                      start=True, stop=True)
                    mm.then_inc(self.sS, 1)
                    if idx == 0 and pendR is not None:
                        emit_R(pendR)
                        pendR = None
                    if idx >= 1:
                        self._spv(t, insts[idx - 1], mc)
                self._spv(t, insts[-1], mc)
                pendR = (mc, insts[-1] % 3)
            if m + 2 < NP:
                self._qt_proj(t, m + 2)
        emit_R(pendR)
        assert all(self.MCinst[(mc // 4, mc % 4)][-1] % 3 != 1 for mc in range(4 * NP))
        t.wait_ge(self.sScat, 16 * NTT * E)
        t.wait_ge(self.sIdxL, 16)
        t.wait_ge(self.sExp, 2 * len(self.AI))
        t.wait_ge(self.sQTe, 4 * NP)
        t.wait_ge(self.sRe, 4 * (NP - 1) + 1)
        t.transpose(self.ps(0, 64), self.idxld[:, :],
                    self.id32[0:64, 0:64]).then_inc(self.sIdxT, 1)
        self.wld(t, ("woT",))
        for tt in range(NTT):
            cq = tt // 4
            t.wait_ge(self.sNorm, 29 + cq)
            t.wait_ge(self.sOTd, 16 * (29 + cq))
            if tt == 0:
                t.wait_ge(self.sIdxE, 1)
                t.wait_ge(self.sRe, 4 * (NP - 1) + 1)
            if tt >= 2:
                t.wait_ge(self.sWoV, tt - 1)
            for ic in range(8):
                for nh in range(2):
                    mm = t.matmul(self.ps(1024 * (tt % 2) + nh * 512, 512),
                                  self.OT[ic][:, tt * 128:(tt + 1) * 128],
                                  self.woT[ic][:, nh * 512:(nh + 1) * 512],
                                  start=(ic == 0), stop=(ic == 7))
            mm.then_inc(self.sWoT, 1)
        t.wait_ge(self.sWoV, NTT)
        t.wait_ge(self.sIdxE, 1)
        for e in range(E):
            for k in range(8):
                t.wait_ge(self.sGa, 16 * (self.gaidx[("xg", e, k)] + 1))
                if not (e == 0 and k < 2):
                    sem = self.sXgE if k % 2 == 0 else self.sXgEv
                    t.wait_ge(sem, 4 * e + k // 2)
                if e > 0 and k < 2:
                    t.wait_ge(self.sSil, 32 * e)
                for dc in range(8):
                    mm = t.transpose(self.psb(512 * (k % 2), 1024)[:, dc * 128:(dc + 1) * 128],
                                     self.XG[k][:, dc * 128:(dc + 1) * 128],
                                     self.idbf[:, :])
                mm.then_inc(self.sXgT, 1)
            for jt in range(16):
                a = self.a2(e, jt)
                self.wld(t, ("w1s", e, jt))
                self.wld(t, ("w3s", e, jt))
                if jt == 0:
                    t.wait_ge(self.sXgE, 4 * (e + 1))
                    t.wait_ge(self.sXgEv, 4 * (e + 1))
                if a >= 2:
                    t.wait_ge(self.sSil, 2 * (a - 2) + 2)
                    t.wait_ge(self.sHTm, 2 * (a - 2) + 2)
                if jt == 1 and e >= 1:
                    t.wait_ge(self.sYsc, 8 * e)
                if jt == 1 and e == 0:
                    t.wait_ge(self.sRe, 4 * NP)
                    t.wait_ge(self.sNorm, 4 * NP)
                pb = 2048 * (jt % 2)
                w1 = self.W1S[jt % 2]
                w3 = self.W3S[jt % 2]
                for dc in range(8):
                    for tcl in range(2):
                        mm = t.matmul(self.ps(pb + 512 * tcl, 512),
                                      w1[:, dc * 128:(dc + 1) * 128],
                                      self.XGT[e % 2][:, dc * 1024 + tcl * 512:
                                                      dc * 1024 + (tcl + 1) * 512],
                                      start=(dc == 0), stop=(dc == 7))
                        if dc == 7:
                            mm.then_inc(self.sH1, 1)
                for dc in range(8):
                    for tcl in range(2):
                        mm = t.matmul(self.ps(pb + 1024 + 512 * tcl, 512),
                                      w3[:, dc * 128:(dc + 1) * 128],
                                      self.XGT[e % 2][:, dc * 1024 + tcl * 512:
                                                      dc * 1024 + (tcl + 1) * 512],
                                      start=(dc == 0), stop=(dc == 7))
                        if dc == 7:
                            mm.then_inc(self.sH3, 1)
            self.wld(t, ("w2", e))
            t.wait_ge(self.sHTm, 32 * (e + 1))
            t.wait_ge(self.sSil, 32 * (e + 1))
            for k in range(8):
                b = self.b2(e, k)
                if b >= 2:
                    t.wait_ge(self.sYsc, b - 1)
                for hc in range(16):
                    for nh in range(2):
                        mm = t.matmul(self.ps(2048 + 1024 * (b % 2) + nh * 512, 512),
                                      self.HT[hc][:, k * 128:(k + 1) * 128],
                                      self.W2[hc][:, nh * 512:(nh + 1) * 512],
                                      start=(hc == 0), stop=(hc == 15))
                mm.then_inc(self.sB, 1)

    def _spv(self, t, i, mc):
        inst = self.AI[i]
        m, c, h = inst['m'], inst['c'], inst['h']
        kv = PAIRS[m][h] // 4
        for j, k in enumerate(inst['kops']):
            ak = self.AK[k]
            t.wait_ge(self.sExp, k + 1)
            if ak['pv_first'] and mc >= 1:
                t.wait_ge(self.sNorm, mc)
            qrel = (ak['qlo'] - 4 * c) * 128
            w = (ak['qhi'] - ak['qlo'] + 1) * 128
            ob = 3072 + 512 * h
            pt = self.PTA[h][ak['kc'] % 4][:, qrel:qrel + w]
            mm = t.matmul(self.PS[0:65, ob + qrel: ob + qrel + w],
                          self.Vt[ak['kc']][:, 65 * kv:65 * kv + 65], pt,
                          start=ak['gfirst'], stop=ak['glast'])
            mm.then_inc(self.sPVk, 1)

    def _gate_mm(self, t, i):
        tt, dc = i // 8, i % 8
        t.wait_ge(self.sTrEvV, i + 1)
        if dc == 0 and tt >= 2:
            t.wait_ge(self.sGe, tt - 1)
        mm = t.matmul(self.ps(1024 + 512 * (tt % 2), 8), self.xnT32[dc % 2],
                      self.wgt[:, dc * 8:(dc + 1) * 8],
                      start=(dc == 0), stop=(dc == 7))
        mm.then_inc(self.sGmm, 1)

    def _qt_proj(self, t, pp):
        self.wld(t, ("wqT", pp))
        if pp == 0:
            t.wait_ge(self.sKTe, 8)
        if pp >= 2:
            t.wait_ge(self.sExp, 2 * (self.pair_lastP0[pp - 2] + 1))
        for tc in range(4):
            t.wait_ge(self.sQTe, 4 * pp + tc)
            for dc in range(8):
                mm = t.matmul(self.ps(0, 512),
                              self.wqT[pp % 2][:, dc * 128:(dc + 1) * 128],
                              self.XNT[:, dc * 2048 + tc * 512: dc * 2048 + (tc + 1) * 512],
                              start=(dc == 0), stop=(dc == 7))
            mm.then_inc(self.sQTt, 1)


def build_nc():
    nc = bass.Bass("TRN2", target_bir_lowering=False, detect_race_conditions=False)
    mk = MK(nc)
    with nc.Block() as block:
        @block.sync
        def _(sy):
            mk.emit_sync(sy)

        @block.gpsimd
        def _(g):
            mk.emit_gpsimd(g)

        @block.scalar
        def _(s):
            mk.emit_scalar(s)

        @block.vector
        def _(v):
            mk.emit_vector(v)

        @block.tensor
        def _(t):
            mk.emit_tensor(t)
    mk.ctx.close()
    return nc


def host_inputs(L0_attn_Wq, L0_attn_Wk, L0_attn_Wv, L0_attn_Wo,
                L0_ffn_w_gate, L0_ffn_w1, L0_ffn_w3, L0_ffn_w2):
    f32 = np.float32
    wq = np.asarray(L0_attn_Wq, f32)
    wk = np.asarray(L0_attn_Wk, f32)
    wv = np.asarray(L0_attn_Wv, f32)
    wo = np.asarray(L0_attn_Wo, f32)
    wgate = np.asarray(L0_ffn_w_gate, f32)
    w1 = np.asarray(L0_ffn_w1, f32)
    w3 = np.asarray(L0_ffn_w3, f32)
    w2 = np.asarray(L0_ffn_w2, f32)

    def chunkmajor(a, rows_per_chunk=128):
        R, Cc = a.shape
        nch = R // rows_per_chunk
        return np.ascontiguousarray(
            a.reshape(nch, rows_per_chunk, Cc).transpose(1, 0, 2).reshape(rows_per_chunk, nch * Cc))

    wqT = wq.T
    wqTp = np.stack([
        chunkmajor(np.concatenate([wqT[:, lo * 64:(lo + 1) * 64],
                                   wqT[:, hi * 64:(hi + 1) * 64]], axis=1))
        for lo, hi in PAIRS])
    wkT = chunkmajor(wk.T)
    wvT = chunkmajor(wv.T)
    perm = np.concatenate([np.r_[lo * 64:(lo + 1) * 64, hi * 64:(hi + 1) * 64]
                           for lo, hi in PAIRS])
    woTp = chunkmajor(np.ascontiguousarray(wo.T[perm, :]))
    wg_r = chunkmajor(wgate)
    w1s = np.ascontiguousarray(
        w1.reshape(E, 8, 128, 16, 128).transpose(0, 3, 2, 1, 4).reshape(E, 16, 128, 1024))
    w3s = np.ascontiguousarray(
        w3.reshape(E, 8, 128, 16, 128).transpose(0, 3, 2, 1, 4).reshape(E, 16, 128, 1024))
    w2r = np.ascontiguousarray(
        w2.reshape(E, 16, 128, 1024).transpose(0, 2, 1, 3).reshape(E, 128, 16384))

    ii = np.arange(128)
    maskc = np.where(ii[:, None] <= ii[None, :], 0.0, -100.0).astype(f32)
    maska = np.where(ii[:, None] > ii[None, :], 0.0, -100.0).astype(f32)
    triu = np.where(ii[:, None] < ii[None, :], 1.0, 0.0).astype(f32)
    tokid = np.zeros((128, 128), f32)
    for tt in range(NTT):
        for e in range(E):
            tokid[:, tt * 8 + e] = tt * 128 + ii
    eCr = np.tile((np.arange(E, dtype=f32) * C)[None, :], (1, NTT)).reshape(1, 128)
    return {
        "wqT": wqTp, "wkT": wkT, "wvT": wvT, "woT": woTp, "wg": wg_r,
        "w1s": w1s, "w3s": w3s, "w2": w2r,
        "id32": np.eye(128, dtype=f32), "idbf": np.eye(128, dtype=ml_dtypes.bfloat16),
        "maskc": maskc, "maska": maska, "triu": triu, "tokid": tokid, "eCr": eCr,
    }


def kernel(L0_attn_x, L0_attn_Wq, L0_attn_Wk, L0_attn_Wv, L0_attn_Wo,
           L0_ffn_w_gate, L0_ffn_w1, L0_ffn_w3, L0_ffn_w2):
    x = np.asarray(L0_attn_x, np.float32)
    shared = host_inputs(L0_attn_Wq, L0_attn_Wk, L0_attn_Wv, L0_attn_Wo,
                         L0_ffn_w_gate, L0_ffn_w1, L0_ffn_w3, L0_ffn_w2)
    nc = build_nc()
    in_maps = []
    for c in range(8):
        m = dict(shared)
        m["x"] = np.ascontiguousarray(x[c])
        in_maps.append(m)
    res = run_bass_kernel_spmd(nc, in_maps, core_ids=list(range(8)))
    return np.stack([r["out"] for r in res.results], axis=0)


# version 25
# speedup vs baseline: 1.1210x; 1.1210x over previous
import os
import numpy as np
import ml_dtypes
import contextlib
import concourse.bass as bass
import concourse.mybir as mybir
from concourse.bass import IndirectOffsetOnAxis
from concourse.bass_utils import run_bass_kernel_spmd

F32 = mybir.dt.float32
BF16 = mybir.dt.bfloat16
I32 = mybir.dt.int32
AF = mybir.ActivationFunctionType
ALU = mybir.AluOpType
AX = mybir.AxisListType

T = 2048
D = 1024
NH = 16
HD = 64
NP = 8
H = 2048
E = 8
C = 1024
NTT = T // 128
EPS = 1e-5
SCALE = 0.125
CAP = 30.0

PAIRS = [(0, 4), (1, 5), (2, 6), (3, 7), (8, 12), (9, 13), (10, 14), (11, 15)]
DBGS = int(os.environ.get("MK_DBGS", "0"))


def _win(qt):
    kc0 = max(0, qt - 8)
    return kc0, qt - kc0 + 1


class MK:
    def __init__(self, nc):
        self.nc = nc
        self.ctx = contextlib.ExitStack()
        n = self.ctx.enter_context

        d = nc.dram_tensor
        self.d_x = d("x", [T, D], F32, kind="ExternalInput").ap()
        self.d_wqT = d("wqT", [NP, 128, 1024], F32, kind="ExternalInput").ap()
        self.d_wkT = d("wkT", [128, 2048], F32, kind="ExternalInput").ap()
        self.d_wvT = d("wvT", [128, 2048], F32, kind="ExternalInput").ap()
        self.d_woT = d("woT", [128, 8192], F32, kind="ExternalInput").ap()
        self.d_wg = d("wg", [128, 64], F32, kind="ExternalInput").ap()
        self.d_w1s = d("w1s", [E, 16, 128, 1024], F32, kind="ExternalInput").ap()
        self.d_w3s = d("w3s", [E, 16, 128, 1024], F32, kind="ExternalInput").ap()
        self.d_w2 = d("w2", [E, 128, 16384], F32, kind="ExternalInput").ap()
        self.d_id32 = d("id32", [128, 128], F32, kind="ExternalInput").ap()
        self.d_idbf = d("idbf", [128, 128], BF16, kind="ExternalInput").ap()
        self.d_maskc = d("maskc", [128, 128], F32, kind="ExternalInput").ap()
        self.d_maska = d("maska", [128, 128], F32, kind="ExternalInput").ap()
        self.d_triu = d("triu", [128, 128], F32, kind="ExternalInput").ap()
        self.d_tokid = d("tokid", [128, 128], F32, kind="ExternalInput").ap()
        self.d_eCr = d("eCr", [1, 128], F32, kind="ExternalInput").ap()
        self.d_out = d("out", [T, D], F32, kind="ExternalOutput").ap()
        if DBGS:
            self.d_dbgi = d("dbgi", [128, 64], I32, kind="ExternalOutput").ap()
            self.d_dbgp = d("dbgp", [128, 128], I32, kind="ExternalOutput").ap()
            self.d_dbgg = d("dbgg", [128, 1024], BF16, kind="ExternalOutput").ap()
            self.d_dbgm2 = d("dbgm2", [128, 64], F32, kind="ExternalOutput").ap()
            self.d_dbgr = d("dbgr", [64, 128], F32, kind="ExternalOutput").ap()
        self.d_xnat = d("xnat", [T + 1, D], BF16).ap()
        self.dh_mtab = d("mtab", [T + 1, E], F32)
        self.d_mtab = self.dh_mtab.ap()
        self.d_idx = d("idxl", [E * C, 1], F32).ap()
        self.d_yacc = d("yacc", [T + 1, D], F32).ap()

        sb = lambda name, shape, dt: n(nc.sbuf_tensor(name, shape, dt))
        self.ACC = sb("ACCb", [128, NTT * 1024], F32)
        self.XNT = sb("XNTb", [128, 8 * 2048], BF16)
        self.AR = sb("ARb", [128, 51200], BF16)
        self.id32 = sb("id32b", [128, 128], F32)
        self.idbf = sb("idbfb", [128, 128], BF16)
        self.maskc = sb("maskcb", [128, 128], F32)
        self.maska = sb("maskab", [128, 128], F32)
        self.triu = sb("triub", [128, 128], F32)
        self.tokid = sb("tokidb", [128, 128], F32)
        self.wgt = sb("wgtb", [128, 64], F32)
        self.stats = sb("statsb", [128, 128], F32)
        self.Lsb = sb("Lsbb", [128, 128], F32)
        self.eq1 = sb("eq1b", [128, 128], F32)
        self.eq2 = sb("eq2b", [128, 128], F32)
        self.msb = sb("msbb", [128, 128], F32)
        self.l2t = sb("l2tb", [128, 8], F32)
        self.p12 = sb("p12b", [128, 32], F32)
        self.sums = sb("sumsb", [128, 4], F32)
        self.csts = sb("cstsb", [128, 2], F32)
        self.junk2 = sb("junk2b", [128, 8], F32)
        self.onesb = sb("onesbb", [128, 1], BF16)
        self.onesp = sb("onespb", [128, 64], BF16)
        self.zbuf = sb("zbufb", [128, 512], F32)
        self.onesc = sb("onescb", [128, 1], F32)
        self.onesr = sb("onesrb", [1, 128], F32)
        self.totsb = sb("totsbb", [1, 128], F32)
        self.basesb = sb("basesbb", [1, 128], F32)
        self.eCr = sb("eCrb", [1, 128], F32)
        self.idxld = sb("idxldb", [64, 128], F32)
        self.idx = sb("idxb", [128, 64], I32)
        self.ppint = sb("ppintb", [128, 128], I32)
        self.mgA = [sb(f"mgA{i}", [128, 64], F32) for i in range(2)]

        def av(off, cols, dt=BF16):
            if dt == BF16:
                return self.AR[:, off // 2: off // 2 + cols]
            assert dt == F32
            return self.AR[:, off // 2: off // 2 + 2 * cols].bitcast(F32)

        self.KT = [av(0 + i * 4096, 2048) for i in range(2)]
        self.Vt = [av(8192 + tt * 528 if tt < 15 else 96768, 260)
                    for tt in range(NTT)]
        self.OT = [av(16384 + m * 4096, 2048) for m in range(NP)]
        self.xn_scr = [av(16384 + i * 4096, 1024, F32) for i in range(2)]
        self.xnT32 = [av(24576 + i * 512, 128, F32) for i in range(2)]
        self.junk = av(32768, 1024, F32)
        self.woT = [av(49152 + ic * 2048, 1024) for ic in range(8)]
        self.wkT = av(65536, 2048)
        self.wvT = av(69632, 2048)
        self.wqT = [av(73728 + i * 2048, 1024) for i in range(2)]
        self.QT = [av(77824 + i * 4096, 2048) for i in range(2)]
        self.PTA = [[av(86016 + (h * 4 + r) * 1024, 512) for r in range(4)]
                    for h in range(2)]
        rb = av(94720, 1024)
        self.rA = rb[64:65, 0:512]
        self.rB = rb[64:65, 512:1024]
        self.tmpB = [rb[0:64, 0:512], rb[0:64, 512:1024],
                     av(97792, 512)[0:64, :], av(101376, 512)[0:64, :]]
        rs = av(98816, 1024)
        self.RsbA = rs[0:64, 0:512]
        self.RsbB = rs[0:64, 512:1024]
        self.l2s = av(94208, 128, F32)

        self.HT = [av(0 + jt * 2048, 1024) for jt in range(16)]
        self.W2 = [av(32768 + hc * 2048, 1024) for hc in range(16)]
        self.W1S = [av(65536 + i * 2048, 1024) for i in range(2)]
        self.W3S = [av(69632 + i * 2048, 1024) for i in range(2)]
        self.SIL = [av(73728 + i * 2048, 512, F32) for i in range(2)]
        self.YSC = [av(77824 + i * 4096, 1024, F32) for i in range(2)]
        self.XG = [av(86016 + k * 2048, 1024) for k in range(8)]
        self.XGT = [self.XNT[:, 0:8192], self.XNT[:, 8192:16384]]

        self.PS = n(nc.psum_tensor("PSall", [128, 4096], F32))
        self.ps = lambda c0, cols: self.PS[:, c0:c0 + cols]
        self.psb = lambda c0, cols: self.PS.ap().bitcast(BF16)[:, 2 * c0: 2 * c0 + cols]

        sems = ("sX sC sCm sWg sNM sSq sVar sRstd sXn sTr1 sTrEvV sTrEvS sGmm sGe "
                "sSelfV sSig sM sKTt sKTe sVt sVe sQTt sQTe sS sTanh sMask sExp sOTd "
                "sX2 sSm sRe sRcp2 sRb sNorm sPVk sWoT sWoV sH1 sH3 sSil sHTm sB "
                "sSelfS sTouch sOut "
                "sSel sTot sTotE sBase sPP sPPe sMdone sMt sZs sXd sScat sIdxL "
                "sIdxT sIdxE sGa sXgT sXgE sXgEv sYsc sYs sYal sFin sXnD sYsD").split()
        for s in sems:
            setattr(self, s, n(nc.semaphore(s)))
        self.selfv = 0

        self.loads = []
        self.lidx = {}
        def L(key):
            self.lidx[key] = len(self.loads)
            self.loads.append(key)
        L(("wkT",)); L(("wvT",)); L(("woT",))
        L(("wqT", 0)); L(("wqT", 1))
        L(("fills",))
        for tt in range(NTT):
            L(("xnd", tt))
        L(("xnz",))
        for m in range(2, NP):
            L(("wqT", m))
        for tt in range(NTT):
            for e in range(E):
                L(("scat", tt, e))
        L(("idxrb",))
        L(("mtd",))
        for jt in range(2):
            L(("w1s", 0, jt)); L(("w3s", 0, jt))
        for k in range(8):
            L(("xg", 0, k)); L(("mg", 0, k))
        for jt in range(2, 8):
            L(("w1s", 0, jt)); L(("w3s", 0, jt))
        for k in range(8):
            L(("xg", 1, k)); L(("mg", 1, k))
        for jt in range(8, 16):
            L(("w1s", 0, jt)); L(("w3s", 0, jt))
        L(("w2", 0))
        for e in range(1, E):
            for jt in range(2):
                L(("w1s", e, jt)); L(("w3s", e, jt))
            for k in range(8):
                L(("ysc", e - 1, k))
            if e + 1 < E:
                for k in range(8):
                    L(("xg", e + 1, k)); L(("mg", e + 1, k))
            for jt in range(2, 16):
                L(("w1s", e, jt)); L(("w3s", e, jt))
            L(("w2", e))
        for k in range(8):
            L(("ysc", E - 1, k))
        for tt in range(NTT):
            L(("yrb", tt))

        self.wgidx = {}
        wi = 0
        for key in self.loads:
            if key[0] in ("wkT", "wvT", "woT", "wqT", "w1s", "w3s", "w2"):
                self.wgidx[key] = wi
                wi += 1
        self.gaidx = {}
        gi = 0
        for key in self.loads:
            if key[0] in ("xg", "mg"):
                self.gaidx[key] = gi
                gi += 1

        self.AI = []
        self.AK = []
        self.MCinst = {}
        self.glastK = {}
        slot_last = {}
        mcount = 0
        for m in range(NP):
            for c in range(4):
                kcl = list(range(max(0, 4 * c - 8), 4 * c + 4))
                nsets = len(kcl) // 2
                sets = [(kcl[2 * t], kcl[2 * t + 1]) for t in range(nsets)]
                tf = next(t for t, p in enumerate(sets) if p[0] == 4 * c)
                order = [sets[tf]] + [p for t, p in enumerate(sets) if t != tf]
                mcl = []
                for h in (0, 1):
                    for t, pair in enumerate(order):
                        i = len(self.AI)
                        kops = []
                        for j, kc in enumerate(pair):
                            qlo = max(4 * c, kc)
                            qhi = min(4 * c + 3, kc + 8)
                            if 4 * c <= kc <= 4 * c + 3:
                                mask, maskq = 'd', kc - 4 * c
                            elif kc <= 4 * c - 5:
                                mask, maskq = 'a', kc + 8 - 4 * c
                            else:
                                mask, maskq = None, None
                            if mask:
                                mcount += 1
                            k = len(self.AK)
                            self.AK.append(dict(
                                k=k, i=i, j=j, m=m, c=c, h=h, kc=kc, qlo=qlo, qhi=qhi,
                                mask=mask, maskq=maskq, mcount=mcount,
                                slotprev=slot_last.get((h, kc % 4)),
                                gfirst=(t == 0 and j == 0),
                                glast=(t == nsets - 1 and j == 1),
                                pv_first=(h == 0 and t == 0 and j == 0),
                                pv_last=(h == 1 and t == nsets - 1 and j == 1)))
                            slot_last[(h, kc % 4)] = k
                            if t == nsets - 1 and j == 1:
                                self.glastK[(m, c, h)] = k
                            kops.append(k)
                        self.AI.append(dict(m=m, c=c, h=h, kcs=pair, i=i, kops=kops))
                        mcl.append(i)
                self.MCinst[(m, c)] = mcl
        self.pair_last = {}
        self.pair_lastP0 = {}
        for inst in self.AI:
            self.pair_last[inst['m']] = inst['i']
            if inst['i'] % 3 == 0:
                self.pair_lastP0[inst['m']] = inst['i']

    def wld(self, eng, key):
        eng.wait_ge(self.sWg, 16 * (self.wgidx[key] + 1))

    @staticmethod
    def u3(m, qt, h):
        return (m * 16 + qt) * 2 + h

    @staticmethod
    def a2(e, jt):
        return e * 16 + jt

    @staticmethod
    def b2(e, k):
        return e * 8 + k

    def emit_sync(self, sy):
        for tt in range(NTT):
            sy.dma_start(self.ACC[:, tt * 1024:(tt + 1) * 1024],
                         self.d_x[tt * 128:(tt + 1) * 128, :]).then_inc(self.sX, 16)
        sy.dma_start(self.id32[:, :], self.d_id32).then_inc(self.sC, 16)
        sy.dma_start(self.idbf[:, :], self.d_idbf).then_inc(self.sC, 16)
        sy.dma_start(self.maskc[:, :], self.d_maskc).then_inc(self.sC, 16)
        sy.dma_start(self.maska[:, :], self.d_maska).then_inc(self.sC, 16)
        sy.dma_start(self.wgt[:, :], self.d_wg).then_inc(self.sC, 16)
        sy.dma_start(self.triu[:, :], self.d_triu).then_inc(self.sC, 16)
        sy.dma_start(self.tokid[:, :], self.d_tokid).then_inc(self.sC, 16)
        sy.dma_start(self.eCr[:, :], self.d_eCr).then_inc(self.sC, 16)
        for mc in range(4 * NP):
            m, c = divmod(mc, 4)
            sy.wait_ge(self.sNorm, mc + 1)
            sy.dma_start(self.OT[m][64:128, c * 512:(c + 1) * 512],
                         self.tmpB[mc % 4]).then_inc(self.sOTd, 16)

        if DBGS:
            sy.wait_ge(self.sIdxE, 1)
            sy.dma_start(self.d_dbgi, self.idx[:, :]).then_inc(self.sOut, 16)
            sy.dma_start(self.d_dbgr, self.idxld[:, :]).then_inc(self.sOut, 16)
            sy.wait_ge(self.sPPe, 1)
            sy.dma_start(self.d_dbgp, self.ppint[:, :]).then_inc(self.sOut, 16)
            sy.wait_ge(self.sGa, 16 * 2)
            sy.dma_start(self.d_dbgg, self.XG[0]).then_inc(self.sOut, 16)
            sy.wait_ge(self.sGa, 16 * 16)
            sy.dma_start(self.d_dbgm2, self.mgA[0][:, :]).then_inc(self.sOut, 16)
        sy.wait_ge(self.sZs, 16 * 36)
        for tt in range(NTT):
            sy.wait_ge(self.sFin, tt + 1)
            sy.wait_ge(self.sTouch, tt + 1)
            sy.dma_start(self.d_out[tt * 128:(tt + 1) * 128, :],
                         self.ACC[:, tt * 1024:(tt + 1) * 1024]).then_inc(self.sOut, 16)
        sy.wait_ge(self.sOut, 16 * (NTT + (5 if DBGS else 0)))

    def emit_gpsimd(self, g):
        bnd = []
        for e in range(E):
            r = g.alloc_register(f"bnd{e}")
            g.reg_mov(r, (e + 1) * C - 1)
            bnd.append(r)
        for key in self.loads:
            kind = key[0]
            if kind == "wkT":
                g.dma_start(self.wkT, self.d_wkT).then_inc(self.sWg, 16)
            elif kind == "wvT":
                g.dma_start(self.wvT, self.d_wvT).then_inc(self.sWg, 16)
            elif kind == "woT":
                g.dma_start(self.AR[:, 49152 // 2: 49152 // 2 + 8192],
                            self.d_woT).then_inc(self.sWg, 16)
            elif kind == "wqT":
                m = key[1]
                if m >= 2:
                    g.wait_ge(self.sQTt, 4 * (m - 1))
                g.dma_start(self.wqT[m % 2], self.d_wqT[m]).then_inc(self.sWg, 16)
            elif kind == "xnd":
                tt = key[1]
                g.wait_ge(self.sXnD, tt + 1)
                g.dma_start(self.d_xnat[tt * 128:(tt + 1) * 128, :],
                            self.xn_scr[tt % 2]).then_inc(self.sXd, 16)
            elif kind == "xnz":
                g.wait_ge(self.sCm, 8)
                g.dma_start(self.d_xnat[T:T + 1, 0:512],
                            self.zbuf[0:1, 0:512]).then_inc(self.sXd, 16)
                g.dma_start(self.d_xnat[T:T + 1, 512:1024],
                            self.zbuf[0:1, 0:512]).then_inc(self.sXd, 16)
            elif kind == "scat":
                tt, e = key[1], key[2]
                if (tt, e) == (0, 0):
                    g.wait_ge(self.sPPe, 1)
                    g.wait_ge(self.sZs, 16)
                col = tt * 8 + e
                g.indirect_dma_start(
                    self.d_idx,
                    IndirectOffsetOnAxis(self.ppint[:, col:col + 1], 0),
                    self.tokid[:, col:col + 1], None,
                    bounds_check=bnd[e],
                    oob_is_err=False).then_inc(self.sScat, 16)
            elif kind == "fills":
                g.wait_ge(self.sCm, 7)
                g.dma_start(self.d_idx.rearrange("(p c) one -> p (c one)", p=128),
                            self.zbuf[:, 0:64]).then_inc(self.sZs, 16)
                g.wait_ge(self.sCm, 8)
                for r in range(NTT):
                    for hh in range(2):
                        g.dma_start(self.d_yacc[r * 128:(r + 1) * 128,
                                                hh * 512:(hh + 1) * 512],
                                    self.zbuf[:, :]).then_inc(self.sZs, 16)
                g.dma_start(self.d_yacc[T:T + 1, 0:512],
                            self.zbuf[0:1, :]).then_inc(self.sZs, 16)
                g.dma_start(self.d_yacc[T:T + 1, 512:1024],
                            self.zbuf[0:1, :]).then_inc(self.sZs, 16)
                g.dma_start(self.d_mtab[T:T + 1, :],
                            self.zbuf[0:1, 0:E]).then_inc(self.sZs, 16)
            elif kind == "mtd":
                g.wait_ge(self.sMdone, 1)
                g.dma_start(bass.AP(self.dh_mtab, 0, [[E, 128], [128 * E, NTT], [1, E]]),
                            self.msb[:, :].rearrange("p (tt e) -> p tt e",
                                                     e=E)).then_inc(self.sMt, 16)
            elif kind == "idxrb":
                g.wait_ge(self.sScat, 16 * NTT * E)
                g.dma_start(self.idxld[:, :],
                            self.d_idx.rearrange("(pp c) one -> pp (c one)",
                                                 pp=64)).then_inc(self.sIdxL, 16)
            elif kind == "yrb":
                tt = key[1]
                if tt == 0:
                    g.wait_ge(self.sYs, 16 * 8 * E)
                if tt >= 2:
                    g.wait_ge(self.sFin, tt - 1)
                g.dma_start(self.YSC[tt % 2],
                            self.d_yacc[tt * 128:(tt + 1) * 128, :]).then_inc(self.sYal, 16)
            elif kind == "xg":
                e, k = key[1], key[2]
                if (e, k) == (0, 0):
                    g.wait_ge(self.sIdxE, 1)
                    g.wait_ge(self.sXd, 16 * 18)
                    g.wait_ge(self.sOTd, 16 * 4 * NP)
                if e >= 1:
                    g.wait_ge(self.sXgT, 8 * (e - 1) + k + 1)
                g.indirect_dma_start(
                    self.XG[k], None, self.d_xnat,
                    IndirectOffsetOnAxis(self.idx[:, e * 8 + k: e * 8 + k + 1], 0)
                ).then_inc(self.sGa, 16)
            elif kind == "mg":
                e, k = key[1], key[2]
                if (e, k) == (0, 0):
                    g.wait_ge(self.sMt, 16)
                if e >= 2:
                    g.wait_ge(self.sYsc, 8 * (e - 1))
                g.indirect_dma_start(
                    self.mgA[e % 2][:, k * 8:(k + 1) * 8], None, self.d_mtab,
                    IndirectOffsetOnAxis(self.idx[:, e * 8 + k: e * 8 + k + 1], 0)
                ).then_inc(self.sGa, 16)
            elif kind == "ysc":
                e, k = key[1], key[2]
                b = self.b2(e, k)
                g.wait_ge(self.sYsD, b + 1)
                if k == 0 and e == 0:
                    g.wait_ge(self.sZs, 16 * 36)
                if k == 0 and e >= 1:
                    g.wait_ge(self.sYs, 16 * 8 * e)
                g.indirect_dma_start(
                    self.d_yacc,
                    IndirectOffsetOnAxis(self.idx[:, e * 8 + k: e * 8 + k + 1], 0),
                    self.YSC[b % 2], None,
                    compute_op=ALU.add).then_inc(self.sYs, 16)
            elif kind == "w1s" or kind == "w3s":
                e, jt = key[1], key[2]
                a = self.a2(e, jt)
                if a < 2:
                    g.wait_ge(self.sKTt, 8)
                    g.wait_ge(self.sVt, NTT)
                else:
                    g.wait_ge(self.sH3, 2 * (a - 2) + 2)
                src = self.d_w1s if kind == "w1s" else self.d_w3s
                dst = (self.W1S if kind == "w1s" else self.W3S)[jt % 2]
                g.dma_start(dst, src[e, jt]).then_inc(self.sWg, 16)
            elif kind == "w2":
                e = key[1]
                if e == 0:
                    g.wait_ge(self.sWoT, NTT)
                else:
                    g.wait_ge(self.sB, 8 * e)
                g.dma_start(self.AR[:, 32768 // 2: 32768 // 2 + 16384],
                            self.d_w2[e]).then_inc(self.sWg, 16)

    def emit_scalar(self, s):
        for tt in range(NTT):
            s.wait_ge(self.sNM, tt + 1)
            s.activation(self.junk, self.ACC[:, tt * 1024:(tt + 1) * 1024],
                         AF.Square, bias=self.stats[:, 16 + tt:17 + tt],
                         accum_out=self.stats[:, 32 + tt:33 + tt]).then_inc(self.sSq, 1)
        s.wait_ge(self.sVar, 1)
        s.wait_ge(self.sCm, 2)
        s.activation(self.stats[:, 48:64], self.stats[:, 48:64], AF.Ln,
                     bias=self.csts[:, 0:1]).then_inc(self.sSelfS, 1)
        s.wait_ge(self.sSelfS, 1)
        s.activation(self.stats[:, 48:64], self.stats[:, 48:64], AF.Exp,
                     scale=-0.5).then_inc(self.sRstd, 1)
        s.wait_ge(self.sM, 1)
        s.activation(self.p12[:, 0:16], self.stats[:, 80:96], AF.Sigmoid, scale=-1.0)
        s.activation(self.p12[:, 16:32], self.stats[:, 80:96], AF.Sigmoid).then_inc(self.sSig, 1)
        for p0 in range(0, len(self.AI), 2):
            for i in (p0, p0 + 1):
                inst = self.AI[i]
                s.wait_ge(self.sS, i + 1)
                bb = 1024 * (i % 3)
                for j, k in enumerate(inst['kops']):
                    ak = self.AK[k]
                    c = ak['c']
                    off = bb + j * 512 + (ak['qlo'] - 4 * c) * 128
                    w = (ak['qhi'] - ak['qlo'] + 1) * 128
                    s.activation(self.ps(off, w), self.ps(off, w), AF.Tanh,
                                 scale=SCALE / CAP).then_inc(self.sTanh, 1)
            for i in (p0, p0 + 1):
                inst = self.AI[i]
                bb = 1024 * (i % 3)
                for j, k in enumerate(inst['kops']):
                    ak = self.AK[k]
                    c, h, kc = ak['c'], ak['h'], ak['kc']
                    s.wait_ge(self.sTanh, k + 1)
                    if ak['mask']:
                        s.wait_ge(self.sMask, ak['mcount'])
                    if ak['slotprev'] is not None:
                        s.wait_ge(self.sPVk, ak['slotprev'] + 1)
                    off = bb + j * 512 + (ak['qlo'] - 4 * c) * 128
                    qrel = (ak['qlo'] - 4 * c) * 128
                    w = (ak['qhi'] - ak['qlo'] + 1) * 128
                    s.activation(self.PTA[h][kc % 4][:, qrel:qrel + w],
                                 self.ps(off, w), AF.Exp,
                                 scale=CAP, bias=self.csts[:, 1:2]).then_inc(self.sExp, 1)
        for e in range(E):
            for k in range(0, 8, 2):
                s.wait_ge(self.sXgT, 8 * e + k + 1)
                for dc in range(8):
                    op = s.copy(self.XGT[e % 2][:, dc * 1024 + k * 128: dc * 1024 + (k + 1) * 128],
                                self.psb(512 * (k % 2), 1024)[:, dc * 128:(dc + 1) * 128])
                op.then_inc(self.sXgE, 1)
            for jt in range(16):
                a = self.a2(e, jt)
                for tcl in range(2):
                    s.wait_ge(self.sH1, 2 * a + tcl + 1)
                    if a >= 1:
                        s.wait_ge(self.sHTm, 2 * (a - 1) + tcl + 1)
                    s.activation(self.SIL[tcl], self.ps(2048 * (jt % 2) + 512 * tcl, 512),
                                 AF.Silu).then_inc(self.sSil, 1)

    def emit_vector(self, v):
        def selfwait(op):
            op.then_inc(self.sSelfV, 1)
            self.selfv += 1
            v.wait_ge(self.sSelfV, self.selfv)
        v.memset(self.csts[:, 0:1], EPS)
        v.memset(self.csts[:, 1:2], -CAP)
        v.memset(self.onesc[:, :], 1.0)
        v.memset(self.onesb[:, :], 1.0)
        v.memset(self.onesr[:, :], 1.0)
        v.memset(self.onesp[:, :], 1.0)
        v.memset(self.zbuf[:, :], 2048.0)
        op = v.tensor_copy(self.junk2[:, 0:2], self.zbuf[:, 510:512])
        op.then_inc(self.sCm, 7)
        v.wait_ge(self.sZs, 16)
        v.memset(self.zbuf[:, :], 0.0)
        v.tensor_copy(self.junk2[:, 0:2], self.zbuf[:, 510:512]).then_inc(self.sCm, 1)
        for tt in range(NTT):
            for kv in range(4):
                v.memset(self.Vt[tt][:, 65 * kv + 64:65 * kv + 65], 1.0)
        for tt in range(NTT):
            v.wait_ge(self.sX, 16 * (tt + 1))
            acct = self.ACC[:, tt * 1024:(tt + 1) * 1024]
            op = v.reduce_sum(self.stats[:, tt:tt + 1], acct, AX.X)
            selfwait(op)
            v.tensor_scalar(self.stats[:, 16 + tt:17 + tt], self.stats[:, tt:tt + 1],
                            -1.0 / D, None, ALU.mult).then_inc(self.sNM, 1)
        v.wait_ge(self.sSq, NTT)
        v.tensor_scalar(self.stats[:, 48:64], self.stats[:, 32:48],
                        1.0 / D, None, ALU.mult).then_inc(self.sVar, 1)
        def emit_xn(tt):
            v.wait_ge(self.sRstd, 1)
            if tt >= 2:
                v.wait_ge(self.sTr1, (tt - 2) * 8 + 8)
                v.wait_ge(self.sXd, 16 * (tt - 1))
            acct = self.ACC[:, tt * 1024:(tt + 1) * 1024]
            v.tensor_scalar(self.xn_scr[tt % 2], acct,
                            self.stats[:, 16 + tt:17 + tt], self.stats[:, 48 + tt:49 + tt],
                            ALU.add, ALU.mult).then_inc(self.sXn, 1)
            v.tensor_copy(self.junk2[:, 4:6], self.xn_scr[tt % 2][:, 1022:1024]).then_inc(self.sXnD, 1)
        emit_xn(0)
        emit_xn(1)
        for tt in range(NTT):
            for dc in range(8):
                i = tt * 8 + dc
                v.wait_ge(self.sTr1, i + 1)
                if i >= 2:
                    v.wait_ge(self.sGmm, i - 1)
                v.tensor_copy(self.xnT32[dc % 2], self.ps(512 * (dc % 2), 128)).then_inc(self.sTrEvV, 1)
                v.tensor_copy(self.XNT[:, dc * 2048 + tt * 128: dc * 2048 + (tt + 1) * 128],
                              self.ps(512 * (dc % 2), 128)).then_inc(self.sTrEvS, 1)
            v.wait_ge(self.sGmm, tt * 8 + 8)
            v.tensor_copy(self.Lsb[:, tt * 8:(tt + 1) * 8],
                          self.ps(1024 + 512 * (tt % 2), 8)).then_inc(self.sGe, 1)
            if tt + 2 < NTT:
                emit_xn(tt + 2)
        for tt in range(NTT):
            op = v.reduce_max(self.stats[:, 64 + tt:65 + tt],
                              self.Lsb[:, tt * 8:(tt + 1) * 8], AX.X)
        selfwait(op)
        for tt in range(NTT):
            op = v.tensor_scalar(self.eq1[:, tt * 8:(tt + 1) * 8],
                                 self.Lsb[:, tt * 8:(tt + 1) * 8],
                                 self.stats[:, 64 + tt:65 + tt], None, ALU.is_equal)
        selfwait(op)
        for tt in range(NTT):
            op = v.scalar_tensor_tensor(self.l2s[:, tt * 8:(tt + 1) * 8],
                                        self.eq1[:, tt * 8:(tt + 1) * 8],
                                        -1e30, self.Lsb[:, tt * 8:(tt + 1) * 8],
                                        ALU.mult, ALU.add)
        selfwait(op)
        for tt in range(NTT):
            op = v.reduce_max(self.stats[:, 96 + tt:97 + tt],
                              self.l2s[:, tt * 8:(tt + 1) * 8], AX.X)
        selfwait(op)
        for tt in range(NTT):
            v.tensor_scalar(self.eq2[:, tt * 8:(tt + 1) * 8],
                            self.l2s[:, tt * 8:(tt + 1) * 8],
                            self.stats[:, 96 + tt:97 + tt], None, ALU.is_equal)
        for tt in range(NTT):
            op = v.tensor_sub(self.stats[:, 80 + tt:81 + tt],
                              self.stats[:, 96 + tt:97 + tt], self.stats[:, 64 + tt:65 + tt])
        selfwait(op)
        v.engine_nop().then_inc(self.sM, 1)
        v.wait_ge(self.sSig, 1)
        for tt in range(NTT):
            op = v.tensor_scalar(self.l2t[:, :], self.eq2[:, tt * 8:(tt + 1) * 8],
                                 self.p12[:, 16 + tt:17 + tt], None, ALU.mult)
            selfwait(op)
            op = v.scalar_tensor_tensor(self.msb[:, tt * 8:(tt + 1) * 8],
                                        self.eq1[:, tt * 8:(tt + 1) * 8],
                                        self.p12[:, tt:tt + 1], self.l2t[:, :],
                                        ALU.mult, ALU.add)
        selfwait(v.engine_nop())
        v.tensor_copy(self.junk2[:, 2:4], self.msb[:, 126:128]).then_inc(self.sMdone, 1)
        v.tensor_add(self.eq1[:, :], self.eq1[:, :], self.eq2[:, :]).then_inc(self.sSel, 1)
        for it in range(2):
            for tc in range(4):
                i = it * 4 + tc
                v.wait_ge(self.sKTt, i + 1)
                v.tensor_copy(self.KT[it][:, tc * 512:(tc + 1) * 512],
                              self.ps(512 * (i % 2), 512)).then_inc(self.sKTe, 1)
        for tt in range(NTT):
            v.wait_ge(self.sVt, tt + 1)
            for kv in range(4):
                op = v.tensor_copy(self.Vt[tt][:, 65 * kv:65 * kv + 64],
                                   self.ps(1024 + 512 * (tt % 2), 256)[:, 64 * kv:64 * kv + 64])
            op.then_inc(self.sVe, 1)
        for pp in range(2):
            for tc in range(4):
                v.wait_ge(self.sQTt, 4 * pp + tc + 1)
                v.tensor_copy(self.QT[pp % 2][:, tc * 512:(tc + 1) * 512],
                              self.ps(0, 512)).then_inc(self.sQTe, 1)
        v.wait_ge(self.sTot, 1)
        op = v.tensor_copy(self.totsb[:, :], self.ps(512, 128)[0:1, :])
        selfwait(op)
        op = v.tensor_copy(self.basesb[:, :], self.eCr[:, :])
        selfwait(op)
        for tt in range(1, NTT):
            op = v.tensor_add(self.basesb[0:1, tt * 8:(tt + 1) * 8],
                              self.basesb[0:1, (tt - 1) * 8:tt * 8],
                              self.totsb[0:1, (tt - 1) * 8:tt * 8])
            selfwait(op)
        v.engine_nop().then_inc(self.sBase, 1)
        v.wait_ge(self.sPP, 1)
        op = v.tensor_scalar(self.l2s[:, :], self.ps(0, 128), 1.0e6, None, ALU.add)
        selfwait(op)
        op = v.scalar_tensor_tensor(self.l2s[:, :], self.eq1[:, :], -1.0e6,
                                    self.l2s[:, :], ALU.mult, ALU.add)
        selfwait(op)
        op = v.tensor_copy(self.ppint[:, :], self.l2s[:, :])
        selfwait(op)
        v.reduce_max(self.junk2[:, 7:8], self.ppint[:, :], AX.X).then_inc(self.sPPe, 1)
        for m in range(NP):
            for c in range(4):
                mc = m * 4 + c
                insts = self.MCinst[(m, c)]
                for i in insts:
                    bb = 1024 * (i % 3)
                    for j, k in enumerate(self.AI[i]['kops']):
                        ak = self.AK[k]
                        if not ak['mask']:
                            continue
                        v.wait_ge(self.sTanh, k + 1)
                        msk = self.maskc if ak['mask'] == 'd' else self.maska
                        off = bb + j * 512 + ak['maskq'] * 128
                        v.tensor_add(self.ps(off, 128), self.ps(off, 128),
                                     msk[:, :]).then_inc(self.sMask, 1)
                if c == 3 and m + 2 < NP:
                    pp = m + 2
                    for tc in range(4):
                        v.wait_ge(self.sQTt, 4 * pp + tc + 1)
                        v.wait_ge(self.sS, self.pair_last[pp - 2] + 1)
                        v.tensor_copy(self.QT[pp % 2][:, tc * 512:(tc + 1) * 512],
                                      self.ps(0, 512)).then_inc(self.sQTe, 1)
                obA = 3072
                obB = 3584
                p = insts[-1] % 3
                with self.nc.allow_low_precision(reason="bf16 softmax recip (bar 2.57e-3)"):
                    v.wait_ge(self.sPVk, self.glastK[(m, c, 0)] + 1)
                    v.reciprocal(self.rA, self.PS[64:65, obA:obA + 512])
                    v.wait_ge(self.sPVk, self.glastK[(m, c, 1)] + 1)
                    v.reciprocal(self.rB, self.PS[64:65, obB:obB + 512]).then_inc(self.sRcp2, 1)
                v.wait_ge(self.sRb, mc + 1)
                v.tensor_copy(self.RsbB, self.PS[0:64, 1024 * p + 512:1024 * p + 1024])
                op = v.tensor_copy(self.RsbA, self.PS[0:64, 1024 * p:1024 * p + 512])
                op.then_inc(self.sRe, 1)
                selfwait(v.engine_nop())
                if mc >= 4:
                    v.wait_ge(self.sOTd, 16 * (mc - 3))
                v.tensor_tensor(self.OT[m][0:64, c * 512:(c + 1) * 512],
                                self.PS[0:64, obA:obA + 512], self.RsbA, ALU.mult)
                v.tensor_tensor(self.tmpB[mc % 4], self.PS[0:64, obB:obB + 512],
                                self.RsbB, ALU.mult)
                op = v.tensor_copy(self.junk2[0:64, 0:2], self.tmpB[mc % 4][:, 510:512])
                op.then_inc(self.sNorm, 1)
        v.wait_ge(self.sIdxT, 1)
        op = v.tensor_copy(self.idx[:, :], self.ps(0, 64))
        selfwait(op)
        v.reduce_max(self.junk2[:, 6:7], self.idx[:, :], AX.X).then_inc(self.sIdxE, 1)
        for tt in range(NTT):
            v.wait_ge(self.sWoT, tt + 1)
            acct = self.ACC[:, tt * 1024:(tt + 1) * 1024]
            v.tensor_add(acct, acct, self.ps(1024 * (tt % 2), 1024)).then_inc(self.sWoV, 1)
        for e in range(E):
            for k in range(1, 8, 2):
                v.wait_ge(self.sXgT, 8 * e + k + 1)
                for dc in range(8):
                    op = v.tensor_copy(
                        self.XGT[e % 2][:, dc * 1024 + k * 128: dc * 1024 + (k + 1) * 128],
                        self.psb(512 * (k % 2), 1024)[:, dc * 128:(dc + 1) * 128])
                op.then_inc(self.sXgEv, 1)
            for jt in range(16):
                a = self.a2(e, jt)
                for tcl in range(2):
                    v.wait_ge(self.sSil, 2 * a + tcl + 1)
                    v.wait_ge(self.sH3, 2 * a + tcl + 1)
                    if jt == 0 and tcl == 0 and e > 0:
                        v.wait_ge(self.sB, 8 * e)
                    v.tensor_tensor(self.HT[jt][:, tcl * 512:(tcl + 1) * 512],
                                    self.SIL[tcl],
                                    self.ps(2048 * (jt % 2) + 1024 + 512 * tcl, 512),
                                    ALU.mult).then_inc(self.sHTm, 1)
            for k in range(8):
                b = self.b2(e, k)
                v.wait_ge(self.sB, b + 1)
                if b >= 2:
                    v.wait_ge(self.sYs, 16 * (b - 1))
                v.wait_ge(self.sGa, 16 * (self.gaidx[("mg", e, k)] + 1))
                v.tensor_scalar(self.YSC[b % 2], self.ps(2048 + 1024 * (b % 2), 1024),
                                self.mgA[e % 2][:, k * 8 + e: k * 8 + e + 1], None,
                                ALU.mult).then_inc(self.sYsc, 1)
                v.tensor_copy(self.junk2[:, 4:6], self.YSC[b % 2][:, 1022:1024]).then_inc(self.sYsD, 1)
        for tt in range(NTT):
            v.wait_ge(self.sYal, 16 * (tt + 1))
            acct = self.ACC[:, tt * 1024:(tt + 1) * 1024]
            v.tensor_add(acct, acct, self.YSC[tt % 2]).then_inc(self.sFin, 1)
            v.tensor_copy(self.junk2[:, 2:4], acct[:, 1022:1024]).then_inc(self.sTouch, 1)

    def emit_tensor(self, t):
        t.wait_ge(self.sC, 16 * 8)
        for tt in range(NTT):
            for dc in range(8):
                i = tt * 8 + dc
                t.wait_ge(self.sXn, tt + 1)
                if i >= 2:
                    t.wait_ge(self.sTrEvV, i - 1)
                    t.wait_ge(self.sTrEvS, i - 1)
                t.transpose(self.ps(512 * (dc % 2), 128),
                            self.xn_scr[tt % 2][:, dc * 128:(dc + 1) * 128],
                            self.id32[:, :]).then_inc(self.sTr1, 1)
                if i >= 1:
                    self._gate_mm(t, i - 1)
            if tt == NTT - 1:
                self._gate_mm(t, NTT * 8 - 1)
        t.wait_ge(self.sTrEvV, 128)
        t.wait_ge(self.sTrEvS, 128)
        t.wait_ge(self.sGe, NTT)
        self.wld(t, ("wkT",))
        for it in range(2):
            for tc in range(4):
                i = it * 4 + tc
                if i >= 2:
                    t.wait_ge(self.sKTe, i - 1)
                for dc in range(8):
                    mm = t.matmul(self.ps(512 * (i % 2), 512),
                                  self.wkT[:, dc * 256 + it * 128: dc * 256 + (it + 1) * 128],
                                  self.XNT[:, dc * 2048 + tc * 512: dc * 2048 + (tc + 1) * 512],
                                  start=(dc == 0), stop=(dc == 7))
                mm.then_inc(self.sKTt, 1)
        self.wld(t, ("wvT",))
        for tt in range(NTT):
            if tt >= 2:
                t.wait_ge(self.sVe, tt - 1)
            for dc in range(8):
                mm = t.matmul(self.ps(1024 + 512 * (tt % 2), 256),
                              self.XNT[:, dc * 2048 + tt * 128: dc * 2048 + (tt + 1) * 128],
                              self.wvT[:, dc * 256:(dc + 1) * 256],
                              start=(dc == 0), stop=(dc == 7))
            mm.then_inc(self.sVt, 1)
        for pp in range(2):
            self._qt_proj(t, pp)
        t.wait_ge(self.sSel, 1)
        t.wait_ge(self.sQTe, 8)
        t.matmul(self.PS[0:1, 512:640], self.onesc[:, :], self.eq1[:, :],
                 start=True, stop=True).then_inc(self.sTot, 1)
        t.wait_ge(self.sBase, 1)
        t.matmul(self.ps(0, 128), self.triu[:, :], self.eq1[:, :],
                 start=True, stop=False)
        t.matmul(self.ps(0, 128), self.onesr[:, :], self.basesb[:, :],
                 start=False, stop=True).then_inc(self.sPP, 1)
        t.wait_ge(self.sKTe, 8)
        t.wait_ge(self.sVe, NTT)
        t.wait_ge(self.sCm, 7)
        t.wait_ge(self.sPPe, 1)
        def emit_R(pr):
            mcp, p = pr
            t.wait_ge(self.sRcp2, mcp + 1)
            t.matmul(self.PS[0:64, 1024 * p:1024 * p + 512],
                     self.onesp[64:65, 0:64], self.rA, start=True, stop=True)
            t.matmul(self.PS[0:64, 1024 * p + 512:1024 * p + 1024],
                     self.onesp[64:65, 0:64], self.rB,
                     start=True, stop=True).then_inc(self.sRb, 1)
        pendR = None
        for m in range(NP):
            t.wait_ge(self.sQTe, 4 * min(m + 2, NP))
            ktile = self.KT[0] if m < 4 else self.KT[1]
            for c in range(4):
                mc = m * 4 + c
                insts = self.MCinst[(m, c)]
                for idx, i in enumerate(insts):
                    inst = self.AI[i]
                    if i >= 3:
                        t.wait_ge(self.sExp, 2 * (i - 3) + 2)
                    if idx == 2 and mc >= 1:
                        t.wait_ge(self.sRe, mc)
                    bb = 1024 * (i % 3)
                    base = 64 * inst['h']
                    for j, k in enumerate(inst['kops']):
                        ak = self.AK[k]
                        qlo, qhi, kc = ak['qlo'], ak['qhi'], ak['kc']
                        w = (qhi - qlo + 1) * 128
                        mm = t.matmul(self.ps(bb + j * 512 + (qlo - 4 * c) * 128, w),
                                      ktile[base:base + 64, kc * 128:(kc + 1) * 128],
                                      self.QT[m % 2][base:base + 64, qlo * 128:(qhi + 1) * 128],
                                      start=True, stop=True)
                    mm.then_inc(self.sS, 1)
                    if idx == 0 and pendR is not None:
                        emit_R(pendR)
                        pendR = None
                    if idx >= 2:
                        self._spv(t, insts[idx - 2], mc)
                self._spv(t, insts[-2], mc)
                self._spv(t, insts[-1], mc)
                pendR = (mc, insts[-1] % 3)
            if m + 2 < NP:
                self._qt_proj(t, m + 2)
        emit_R(pendR)
        assert all(self.MCinst[(mc // 4, mc % 4)][-1] % 3 != 1 for mc in range(4 * NP))
        t.wait_ge(self.sScat, 16 * NTT * E)
        t.wait_ge(self.sIdxL, 16)
        t.wait_ge(self.sExp, 2 * len(self.AI))
        t.wait_ge(self.sQTe, 4 * NP)
        t.wait_ge(self.sRe, 4 * (NP - 1) + 1)
        t.transpose(self.ps(0, 64), self.idxld[:, :],
                    self.id32[0:64, 0:64]).then_inc(self.sIdxT, 1)
        self.wld(t, ("woT",))
        for tt in range(NTT):
            cq = tt // 4
            t.wait_ge(self.sNorm, 29 + cq)
            t.wait_ge(self.sOTd, 16 * (29 + cq))
            if tt == 0:
                t.wait_ge(self.sIdxE, 1)
                t.wait_ge(self.sRe, 4 * (NP - 1) + 1)
            if tt >= 2:
                t.wait_ge(self.sWoV, tt - 1)
            for ic in range(8):
                for nh in range(2):
                    mm = t.matmul(self.ps(1024 * (tt % 2) + nh * 512, 512),
                                  self.OT[ic][:, tt * 128:(tt + 1) * 128],
                                  self.woT[ic][:, nh * 512:(nh + 1) * 512],
                                  start=(ic == 0), stop=(ic == 7))
            mm.then_inc(self.sWoT, 1)
        t.wait_ge(self.sWoV, NTT)
        t.wait_ge(self.sIdxE, 1)
        for e in range(E):
            for k in range(8):
                t.wait_ge(self.sGa, 16 * (self.gaidx[("xg", e, k)] + 1))
                if not (e == 0 and k < 2):
                    sem = self.sXgE if k % 2 == 0 else self.sXgEv
                    t.wait_ge(sem, 4 * e + k // 2)
                if e > 0 and k < 2:
                    t.wait_ge(self.sSil, 32 * e)
                for dc in range(8):
                    mm = t.transpose(self.psb(512 * (k % 2), 1024)[:, dc * 128:(dc + 1) * 128],
                                     self.XG[k][:, dc * 128:(dc + 1) * 128],
                                     self.idbf[:, :])
                mm.then_inc(self.sXgT, 1)
            for jt in range(16):
                a = self.a2(e, jt)
                self.wld(t, ("w1s", e, jt))
                self.wld(t, ("w3s", e, jt))
                if jt == 0:
                    t.wait_ge(self.sXgE, 4 * (e + 1))
                    t.wait_ge(self.sXgEv, 4 * (e + 1))
                if a >= 2:
                    t.wait_ge(self.sSil, 2 * (a - 2) + 2)
                    t.wait_ge(self.sHTm, 2 * (a - 2) + 2)
                if jt == 1 and e >= 1:
                    t.wait_ge(self.sYsc, 8 * e)
                if jt == 1 and e == 0:
                    t.wait_ge(self.sRe, 4 * NP)
                    t.wait_ge(self.sNorm, 4 * NP)
                pb = 2048 * (jt % 2)
                w1 = self.W1S[jt % 2]
                w3 = self.W3S[jt % 2]
                for dc in range(8):
                    for tcl in range(2):
                        mm = t.matmul(self.ps(pb + 512 * tcl, 512),
                                      w1[:, dc * 128:(dc + 1) * 128],
                                      self.XGT[e % 2][:, dc * 1024 + tcl * 512:
                                                      dc * 1024 + (tcl + 1) * 512],
                                      start=(dc == 0), stop=(dc == 7))
                        if dc == 7:
                            mm.then_inc(self.sH1, 1)
                for dc in range(8):
                    for tcl in range(2):
                        mm = t.matmul(self.ps(pb + 1024 + 512 * tcl, 512),
                                      w3[:, dc * 128:(dc + 1) * 128],
                                      self.XGT[e % 2][:, dc * 1024 + tcl * 512:
                                                      dc * 1024 + (tcl + 1) * 512],
                                      start=(dc == 0), stop=(dc == 7))
                        if dc == 7:
                            mm.then_inc(self.sH3, 1)
            self.wld(t, ("w2", e))
            t.wait_ge(self.sHTm, 32 * (e + 1))
            t.wait_ge(self.sSil, 32 * (e + 1))
            for k in range(8):
                b = self.b2(e, k)
                if b >= 2:
                    t.wait_ge(self.sYsc, b - 1)
                for hc in range(16):
                    for nh in range(2):
                        mm = t.matmul(self.ps(2048 + 1024 * (b % 2) + nh * 512, 512),
                                      self.HT[hc][:, k * 128:(k + 1) * 128],
                                      self.W2[hc][:, nh * 512:(nh + 1) * 512],
                                      start=(hc == 0), stop=(hc == 15))
                mm.then_inc(self.sB, 1)

    def _spv(self, t, i, mc):
        inst = self.AI[i]
        m, c, h = inst['m'], inst['c'], inst['h']
        kv = PAIRS[m][h] // 4
        for j, k in enumerate(inst['kops']):
            ak = self.AK[k]
            t.wait_ge(self.sExp, k + 1)
            if ak['pv_first'] and mc >= 1:
                t.wait_ge(self.sNorm, mc)
            qrel = (ak['qlo'] - 4 * c) * 128
            w = (ak['qhi'] - ak['qlo'] + 1) * 128
            ob = 3072 + 512 * h
            pt = self.PTA[h][ak['kc'] % 4][:, qrel:qrel + w]
            mm = t.matmul(self.PS[0:65, ob + qrel: ob + qrel + w],
                          self.Vt[ak['kc']][:, 65 * kv:65 * kv + 65], pt,
                          start=ak['gfirst'], stop=ak['glast'])
            mm.then_inc(self.sPVk, 1)

    def _gate_mm(self, t, i):
        tt, dc = i // 8, i % 8
        t.wait_ge(self.sTrEvV, i + 1)
        if dc == 0 and tt >= 2:
            t.wait_ge(self.sGe, tt - 1)
        mm = t.matmul(self.ps(1024 + 512 * (tt % 2), 8), self.xnT32[dc % 2],
                      self.wgt[:, dc * 8:(dc + 1) * 8],
                      start=(dc == 0), stop=(dc == 7))
        mm.then_inc(self.sGmm, 1)

    def _qt_proj(self, t, pp):
        self.wld(t, ("wqT", pp))
        if pp == 0:
            t.wait_ge(self.sKTe, 8)
        if pp >= 2:
            t.wait_ge(self.sExp, 2 * (self.pair_lastP0[pp - 2] + 1))
        for tc in range(4):
            t.wait_ge(self.sQTe, 4 * pp + tc)
            for dc in range(8):
                mm = t.matmul(self.ps(0, 512),
                              self.wqT[pp % 2][:, dc * 128:(dc + 1) * 128],
                              self.XNT[:, dc * 2048 + tc * 512: dc * 2048 + (tc + 1) * 512],
                              start=(dc == 0), stop=(dc == 7))
            mm.then_inc(self.sQTt, 1)


def build_nc():
    nc = bass.Bass("TRN2", target_bir_lowering=False, detect_race_conditions=False)
    mk = MK(nc)
    with nc.Block() as block:
        @block.sync
        def _(sy):
            mk.emit_sync(sy)

        @block.gpsimd
        def _(g):
            mk.emit_gpsimd(g)

        @block.scalar
        def _(s):
            mk.emit_scalar(s)

        @block.vector
        def _(v):
            mk.emit_vector(v)

        @block.tensor
        def _(t):
            mk.emit_tensor(t)
    mk.ctx.close()
    return nc


def host_inputs(L0_attn_Wq, L0_attn_Wk, L0_attn_Wv, L0_attn_Wo,
                L0_ffn_w_gate, L0_ffn_w1, L0_ffn_w3, L0_ffn_w2):
    f32 = np.float32
    wq = np.asarray(L0_attn_Wq, f32)
    wk = np.asarray(L0_attn_Wk, f32)
    wv = np.asarray(L0_attn_Wv, f32)
    wo = np.asarray(L0_attn_Wo, f32)
    wgate = np.asarray(L0_ffn_w_gate, f32)
    w1 = np.asarray(L0_ffn_w1, f32)
    w3 = np.asarray(L0_ffn_w3, f32)
    w2 = np.asarray(L0_ffn_w2, f32)

    def chunkmajor(a, rows_per_chunk=128):
        R, Cc = a.shape
        nch = R // rows_per_chunk
        return np.ascontiguousarray(
            a.reshape(nch, rows_per_chunk, Cc).transpose(1, 0, 2).reshape(rows_per_chunk, nch * Cc))

    wqT = wq.T
    wqTp = np.stack([
        chunkmajor(np.concatenate([wqT[:, lo * 64:(lo + 1) * 64],
                                   wqT[:, hi * 64:(hi + 1) * 64]], axis=1))
        for lo, hi in PAIRS])
    wkT = chunkmajor(wk.T)
    wvT = chunkmajor(wv.T)
    perm = np.concatenate([np.r_[lo * 64:(lo + 1) * 64, hi * 64:(hi + 1) * 64]
                           for lo, hi in PAIRS])
    woTp = chunkmajor(np.ascontiguousarray(wo.T[perm, :]))
    wg_r = chunkmajor(wgate)
    w1s = np.ascontiguousarray(
        w1.reshape(E, 8, 128, 16, 128).transpose(0, 3, 2, 1, 4).reshape(E, 16, 128, 1024))
    w3s = np.ascontiguousarray(
        w3.reshape(E, 8, 128, 16, 128).transpose(0, 3, 2, 1, 4).reshape(E, 16, 128, 1024))
    w2r = np.ascontiguousarray(
        w2.reshape(E, 16, 128, 1024).transpose(0, 2, 1, 3).reshape(E, 128, 16384))

    ii = np.arange(128)
    maskc = np.where(ii[:, None] <= ii[None, :], 0.0, -100.0).astype(f32)
    maska = np.where(ii[:, None] > ii[None, :], 0.0, -100.0).astype(f32)
    triu = np.where(ii[:, None] < ii[None, :], 1.0, 0.0).astype(f32)
    tokid = np.zeros((128, 128), f32)
    for tt in range(NTT):
        for e in range(E):
            tokid[:, tt * 8 + e] = tt * 128 + ii
    eCr = np.tile((np.arange(E, dtype=f32) * C)[None, :], (1, NTT)).reshape(1, 128)
    return {
        "wqT": wqTp, "wkT": wkT, "wvT": wvT, "woT": woTp, "wg": wg_r,
        "w1s": w1s, "w3s": w3s, "w2": w2r,
        "id32": np.eye(128, dtype=f32), "idbf": np.eye(128, dtype=ml_dtypes.bfloat16),
        "maskc": maskc, "maska": maska, "triu": triu, "tokid": tokid, "eCr": eCr,
    }


def kernel(L0_attn_x, L0_attn_Wq, L0_attn_Wk, L0_attn_Wv, L0_attn_Wo,
           L0_ffn_w_gate, L0_ffn_w1, L0_ffn_w3, L0_ffn_w2):
    x = np.asarray(L0_attn_x, np.float32)
    shared = host_inputs(L0_attn_Wq, L0_attn_Wk, L0_attn_Wv, L0_attn_Wo,
                         L0_ffn_w_gate, L0_ffn_w1, L0_ffn_w3, L0_ffn_w2)
    nc = build_nc()
    in_maps = []
    for c in range(8):
        m = dict(shared)
        m["x"] = np.ascontiguousarray(x[c])
        in_maps.append(m)
    res = run_bass_kernel_spmd(nc, in_maps, core_ids=list(range(8)))
    return np.stack([r["out"] for r in res.results], axis=0)
